# version 14
# baseline (speedup 1.0000x reference)
import numpy as np
import ml_dtypes

import concourse.bass as bass
import concourse.mybir as mybir
from concourse.bass_utils import run_bass_kernel_spmd

BF = ml_dtypes.bfloat16
F32 = mybir.dt.float32
BF16 = mybir.dt.bfloat16
AF = mybir.ActivationFunctionType
AL = mybir.AluOpType
AX = mybir.AxisListType

D = 2048
NH = 16
DB = D // 128
FFH = 8192
NPAIR = FFH // 128
EPS_LN = 1e-5
EPS_RMS = float(np.finfo(np.float32).eps)
SCL = float(1.0 / np.float32(np.sqrt(np.float64(128))))
SHIFT = -12.0
NEG = -1.0e30


class Prog:

    def __init__(self):
        self.counts = {}
        self.ops = {k: [] for k in ("pe", "act", "dve", "pool", "sync")}

    def op(self, eng, emit, deps=(), sem=None, inc=1):
        if sem is None:
            sem = {"pe": "pe", "act": "act", "dve": "dve"}[eng]
        prev = self.counts.get(sem, 0)
        self.counts[sem] = prev + inc
        tok = (sem, self.counts[sem])
        self.ops[eng].append((emit, tuple(d for d in deps if d), sem, inc))
        return tok

    def dma(self, eng, sem, emit, deps=()):
        prev = self.counts.get(sem, 0)
        deps = list(deps) + ([(sem, prev)] if prev else [])
        return self.op(eng, emit, deps, sem=sem, inc=16)

    @property
    def sem_names(self):
        return list(self.counts.keys())

    def run(self, engname, eng, sems):
        waited = {}
        for emit, deps, sem, inc in self.ops[engname]:
            for (s, c) in deps:
                if waited.get(s, 0) < c:
                    eng.wait_ge(sems[s], c)
                    waited[s] = c
            emit(eng).then_inc(sems[sem], inc)


def build_nc(T, dbg=False):
    assert T % 256 == 0
    TQ = T // 2
    RR = T // 32
    NCH = TQ // 128
    NT = T // 128
    NHF = max(1, TQ // 512)
    WQ = min(512, TQ)
    NSHALF = 2 if NCH >= 2 else 1
    SHALF = TQ // NSHALF
    NCH_H = SHALF // 128
    SCP = 2 if NCH_H >= 2 else 1
    NSG = NCH_H // SCP

    nc = bass.Bass("TRN2", target_bir_lowering=False, debug=False)

    x_kv = nc.dram_tensor("x_kv", [T, D], F32, kind="ExternalInput").ap()
    x_q = nc.dram_tensor("x_q", [TQ, D], F32, kind="ExternalInput").ap()
    Wq_d = nc.dram_tensor("Wq", [D, D], F32, kind="ExternalInput").ap()
    Wk_d = nc.dram_tensor("Wk", [D, D], F32, kind="ExternalInput").ap()
    Wv_d = nc.dram_tensor("Wv", [D, D], F32, kind="ExternalInput").ap()
    Wo_d = nc.dram_tensor("Wo", [D, D], F32, kind="ExternalInput").ap()
    bq_d = nc.dram_tensor("bq_t", [128, DB], F32, kind="ExternalInput").ap()
    bk_d = nc.dram_tensor("bk_t", [128, NH], F32, kind="ExternalInput").ap()
    bv_d = nc.dram_tensor("bv_r", [1, D], F32, kind="ExternalInput").ap()
    bo_d = nc.dram_tensor("bo_r", [1, D], F32, kind="ExternalInput").ap()
    wgu_d = nc.dram_tensor("wgu", [2 * FFH, D], F32, kind="ExternalInput").ap()
    wp_d = nc.dram_tensor("wp", [D, FFH], F32, kind="ExternalInput").ap()
    mask_d = nc.dram_tensor("mask1", [NCH, 128, TQ], BF16,
                            kind="ExternalInput").ap()
    p2b_d = nc.dram_tensor("pass2b", [128, 1], F32, kind="ExternalInput").ap()
    id_d = nc.dram_tensor("ident", [128, 128], F32, kind="ExternalInput").ap()
    onc_d = nc.dram_tensor("ones_col", [128, 1], BF16,
                           kind="ExternalInput").ap()
    idb_d = nc.dram_tensor("ident_b", [128, 128], BF16,
                           kind="ExternalInput").ap()
    onr_d = nc.dram_tensor("ones_row", [1, 128], F32,
                           kind="ExternalInput").ap()
    out_d = nc.dram_tensor("out", [TQ, D], F32, kind="ExternalOutput").ap()
    kd_ = "ExternalOutput" if dbg else "Internal"
    wbf = nc.dram_tensor("wbf", [4 * D, D], BF16, kind=kd_).ap()
    kdram = nc.dram_tensor("kdram", [128, NH * T], BF16, kind=kd_).ap()
    vdram = nc.dram_tensor("vdram", [NH, 128, T], BF16, kind=kd_).ap()
    accdram = nc.dram_tensor("accdram", [TQ, D], F32, kind=kd_).ap()
    wptdram = nc.dram_tensor("wptdram", [FFH, D], BF16, kind=kd_).ap()
    wgqt = nc.dram_tensor("wgqt", [2 * FFH, D], BF16, kind=kd_).ap()

    SB = nc.SBUF_PARTITION_SIZE_BYTES
    BASE = 16384 + 512

    def nb(shape, dt):
        return int(np.prod(shape[1:])) * (2 if dt == BF16 else 4)

    cur = [BASE]

    def at(name, shape, dt, o=None):
        if o is None:
            o = cur[0]
            cur[0] = o + ((nb(shape, dt) + 31) // 32) * 32
        assert o >= BASE and o + nb(shape, dt) <= SB, (name, o, nb(shape, dt))
        return nc.alloc_sbuf_tensor_at(name, list(shape), dt, offset=o)

    ident = at("ident", [128, 128], F32)
    ident_b = at("ident_b", [128, 128], BF16)
    bq_t = at("bq_t", [128, DB], F32)
    bk_t = at("bk_t", [128, NH], F32)
    p2b = at("p2b", [128, 1], F32)
    onc = at("onc", [128, 1], BF16)
    onr = at("onr", [1, 128], F32)
    bv_e = at("bv_e", [128, D], BF16)
    bo_e = at("bo_e", [128, D], BF16)
    invcol = at("invcol", [128, max(NCH, 8)], F32)
    sh1 = at("sh1", [128, 1], F32)
    C_END = cur[0]
    XT_own = at("XT_own", [128, DB * TQ], BF16)
    A_END = cur[0]
    Qbig = at("Qbig", [128, DB * TQ], BF16)
    E_END = cur[0]
    XT_oth = at("XT_oth", [128, DB * TQ], BF16)
    Obig = at("Obig", [128, NH * TQ], BF16, o=E_END)
    B_END = cur[0]
    XT_q = at("XT_q", [128, DB * TQ], BF16)
    Dm0 = B_END
    mask_sb = at("mask_sb", [128, NCH * TQ], BF16, o=Dm0)
    MSK_END = Dm0 + NCH * TQ * 2
    D_END = cur[0]
    W0 = cur[0]
    Wsb = at("Wsb", [128, DB * D], BF16)
    W_END = cur[0]
    assert W_END <= SB, ("W", W_END, SB)
    o = W0
    Xf = [at("Xf%d" % i, [128, D], F32, o=o + i * D * 4) for i in range(2)]
    o += 2 * D * 4
    Xc = at("Xc", [128, D], F32, o=o); o += D * 4
    XN = [at("XN%d" % i, [128, D], BF16, o=o + i * D * 2) for i in range(2)]
    o += 2 * D * 2
    ttrj = at("ttrj", [128, D], BF16, o=o); o += D * 2
    p1c = at("p1c", [128, 8], F32, o=o); o += 32
    assert o <= W_END
    o = W0
    Kh = [at("Kh%d" % i, [128, T], BF16, o=o + i * T * 2) for i in range(2)]
    o += 2 * T * 2
    Vh = [at("Vh%d" % i, [128, T], BF16, o=o + i * T * 2) for i in range(2)]
    o += 2 * T * 2
    Pt = [at("Pt%d" % i, [128, TQ], BF16, o=o + i * TQ * 2) for i in range(2)]
    o += 2 * TQ * 2
    rs_rcp = at("rs_rcp", [1, TQ], F32, o=o); o += TQ * 4
    rc_sb = at("rc_sb", [128, TQ], F32, o=o); o += TQ * 4
    assert o <= W_END
    hb = NPAIR * SHALF * 2
    HB0 = A_END if A_END + hb <= Dm0 else W_END
    assert HB0 + hb <= SB, (HB0, hb)
    Hbig = at("Hbig", [128, NPAIR * SHALF], BF16, o=HB0)
    HB_END = HB0 + hb
    KA0 = MSK_END if MSK_END + 2 * D * 4 <= W0 else max(W_END, HB_END)
    assert KA0 + 2 * D * 4 <= SB, (KA0,)
    TOP0 = W_END
    kvst = [at("kvst%d" % i, [128, 1024], BF16, o=TOP0 + i * 1024 * 2)
            for i in range(2)]
    accst = [at("accst%d" % i, [128, D], F32, o=KA0 + i * D * 4)
             for i in range(2)]
    HT = D // 2
    qb = [at("qb%d" % i, [128, HT], BF16, o=KA0 + i * HT * 2)
          for i in range(5)]
    qquot = [at("qquot%d" % i, [128, HT], BF16, o=KA0 + (5 + i) * HT * 2)
             for i in range(3)]
    qpt = qquot
    assert KA0 + 8 * HT * 2 <= W0
    QG0 = TOP0 + 2 * 1024 * 2
    qgs = [at("qgs%d" % i, [128, HT // 64], F32, o=QG0 + i * 64)
           for i in range(2)]
    qgc = [at("qgc%d" % i, [128, HT // 64], BF16, o=QG0 + 128 + i * 32)
           for i in range(2)]
    qgr = [at("qgr%d" % i, [128, HT // 64], F32, o=QG0 + 192 + i * 64)
           for i in range(2)]
    assert QG0 + 320 <= SB
    wcst = [at("wcst%d" % i, [128, D], BF16, o=W0 + 14 * D * 2 + i * D * 2)
            for i in range(2)]
    o = W0
    wnat = [at("wnat%d" % i, [128, D], BF16, o=o + i * D * 2)
            for i in range(4)]
    WNAT0 = o
    o += 4 * D * 2
    quot = at("quot", [128, D], BF16, o=o); o += D * 2
    m2b = at("m2b", [128, D], BF16, o=o); o += D * 2
    gsum = at("gsum", [128, D // 64], F32, o=o); o += (D // 64) * 4
    gsc = at("gsc", [128, D // 64], BF16, o=o); o += (D // 64) * 4
    grc = at("grc", [128, D // 64], F32, o=o); o += (D // 64) * 4
    wT = [at("wT%d" % i, [128, D], BF16, o=o + i * D * 2) for i in range(4)]
    WT0 = o
    o += 4 * D * 2
    wpt = [at("wpt%d" % i, [128, D], BF16, o=o + i * D * 2) for i in range(2)]
    o += 2 * D * 2
    silu_sb = at("silu_sb", [128, SHALF], BF16, o=o); o += SHALF * 2
    h2_sb = at("h2_sb", [128, SHALF], BF16, o=o); o += SHALF * 2
    r1row = at("r1row", [1, SHALF], F32, o=o); o += SHALF * 4
    trin = at("trin", [128, 128], F32, o=o); o += 128 * 4
    assert o <= W_END, ("p7", o, W_END)
    B9 = Dm0 if Dm0 + 4 * D * 4 <= W0 else KA0 + 2 * D * 4
    assert B9 + 4 * D * 4 <= SB, (B9,)
    assert B9 >= W0 or B9 + 4 * D * 4 <= W0 or Dm0 + 4 * D * 4 <= W0
    acct = [at("acct%d" % i, [128, D], F32, o=B9 + i * D * 4)
            for i in range(2)]
    xt_ = [at("xt_%d" % i, [128, D], F32, o=B9 + (2 + i) * D * 4)
           for i in range(2)]
    ost = [at("ost%d" % i, [128, D], F32, o=WNAT0 + i * D * 4)
           for i in range(2)]
    wpT4 = [at("wpT%d" % i, [128, D], BF16, o=WT0 + i * D * 2)
            for i in range(4)]

    PS = nc.alloc_psum_tensor("PS", [128, 4096], F32)
    P = Prog()

    t_id = P.dma("sync", "dc", lambda e: e.dma_start(out=ident[:, :], in_=id_d))
    t_idb = P.dma("sync", "dc", lambda e: e.dma_start(out=ident_b[:, :], in_=idb_d))
    t_bq = P.dma("sync", "dc", lambda e: e.dma_start(out=bq_t[:, :], in_=bq_d))
    t_bk = P.dma("sync", "dc", lambda e: e.dma_start(out=bk_t[:, :], in_=bk_d))
    t_p2 = P.dma("sync", "dc", lambda e: e.dma_start(out=p2b[:, :], in_=p2b_d))
    t_onc = P.dma("sync", "dc", lambda e: e.dma_start(out=onc[:, :], in_=onc_d))
    t_onr = P.dma("sync", "dc", lambda e: e.dma_start(out=onr[:, :], in_=onr_d))
    t_bv = P.dma("sync", "dc", lambda e: e.dma_start(out=Xf[0][0:1, :], in_=bv_d))
    t_bo = P.dma("sync", "dc", lambda e: e.dma_start(out=Xf[1][0:1, :], in_=bo_d))
    t_sh1 = P.op("dve", lambda e: e.memset(sh1[:, :], SHIFT))

    bank_dep = [None] * 4
    for src_t, stg, buf in ((t_bv, Xf[0], bv_e), (t_bo, Xf[1], bo_e)):
        for i in range(D // 512):
            bk_i = i % 4
            def em(e, stg=stg, i=i, bk_i=bk_i):
                return e.matmul(PS[:, bk_i * 512:(bk_i + 1) * 512],
                                onr[0:1, :], stg[0:1, i * 512:(i + 1) * 512],
                                start=True, stop=True)
            tmk = P.op("pe", em, deps=[src_t, t_onr, bank_dep[bk_i]])
            def ev(e, buf=buf, i=i, bk_i=bk_i):
                return e.tensor_copy(buf[:, i * 512:(i + 1) * 512],
                                     PS[:, bk_i * 512:(bk_i + 1) * 512])
            bank_dep[bk_i] = P.op("dve", ev, deps=[tmk])
    bvbo_done = [t for t in bank_dep if t]

    prep_srcs = [Wq_d, Wk_d, Wv_d, Wo_d]
    prep_last = [None, None]
    prep_w_done = {}
    prep_state = {"i": 0}

    def prep_step(k=1):
        for _ in range(k):
            i = prep_state["i"]
            if i >= 4 * DB:
                return
            prep_state["i"] = i + 1
            w, t = i // DB, i % DB
            par = i % 2
            tl = P.dma("pool", "dwc%d" % par,
                       (lambda e, w=w, t=t, par=par: e.dma_start(
                           out=wcst[par][:, :],
                           in_=prep_srcs[w][t * 128:(t + 1) * 128, :])),
                       deps=[prep_last[par]])
            ts_ = P.dma("sync", "dws%d" % par,
                        (lambda e, i=i, par=par: e.dma_start(
                            out=wbf[i * 128:(i + 1) * 128, :],
                            in_=wcst[par][:, :])), deps=[tl])
            prep_last[par] = ts_
            prep_w_done[w] = ts_

    n_tiles = NT + NCH
    xbar_toks = []
    ln_dve_last = None
    buf_free = [None, None]
    xn_free = [None, None]

    for i in range(n_tiles):
        par = i % 2
        src = (x_kv[i * 128:(i + 1) * 128, :] if i < NT
               else x_q[(i - NT) * 128:(i - NT + 1) * 128, :])
        tl = P.dma("sync", "dx%d" % par,
                   (lambda e, src=src, par=par:
                    e.dma_start(out=Xf[par][:, :], in_=src)),
                   deps=[buf_free[par]] + (bvbo_done if i < 2 else []))
        t1 = P.op("dve", (lambda e, par=par: e.reduce_sum(
            p1c[:, 0:1], Xf[par][:, :], AX.X)), deps=[tl, ln_dve_last])
        t2 = P.op("dve", lambda e: e.tensor_scalar(
            p1c[:, 1:2], p1c[:, 0:1], 1.0 / D, None, AL.mult), deps=[t1])
        t3 = P.op("dve", (lambda e, par=par: e.tensor_scalar(
            Xc[:, :], Xf[par][:, :], p1c[:, 1:2], None, AL.subtract)),
            deps=[t2])
        buf_free[par] = t3
        t4 = P.op("act", lambda e: e.activation(
            ttrj[:, :], Xc[:, :], AF.Square, accum_out=p1c[:, 2:3]),
            deps=[t3])
        t5 = P.op("dve", lambda e: e.tensor_scalar(
            p1c[:, 3:4], p1c[:, 2:3], 1.0 / D, EPS_LN, AL.mult, AL.add),
            deps=[t4])
        t6 = P.op("act", lambda e: e.activation(
            p1c[:, 4:5], p1c[:, 3:4], AF.Sqrt), deps=[t5])
        t7 = P.op("dve", lambda e: e.reciprocal(
            p1c[:, 5:6], p1c[:, 4:5]), deps=[t6])
        t8 = P.op("dve", (lambda e, par=par: e.tensor_scalar(
            XN[par][:, :], Xc[:, :], p1c[:, 5:6], None, AL.mult)),
            deps=[t7, xn_free[par]])
        ln_dve_last = t8
        if i < NCH:
            dsti, col = XT_own, i * 128
        elif i < NT:
            dsti, col = XT_oth, (i - NCH) * 128
        else:
            dsti, col = XT_q, (i - NT) * 128
        def tp(e, par=par, dsti=dsti, col=col):
            dst = dsti[:, :].rearrange("p (k s) -> p k s", k=DB)[:, :,
                                                                col:col + 128]
            return e.dma_start(out=dst, in_=XN[par][:, :], transpose=True)
        tk = P.dma("act", "dt%d" % par, tp, deps=[t8])
        xn_free[par] = tk
        xbar_toks.append(tk)
        prep_step(3)

    while prep_state["i"] < 4 * DB:
        prep_step(1)

    xt_own_done = xbar_toks[NCH - 1]
    xt_oth_done = xbar_toks[NT - 1]
    xt_q_done = xbar_toks[NT + NCH - 1]
    p1_done = [xbar_toks[-1], xbar_toks[-2], ln_dve_last]

    wq_loads = []
    for k in range(DB):
        tk = P.dma("sync", "dw%d" % (k % 2), (lambda e, k=k: e.dma_start(
            out=Wsb[:, k * D:(k + 1) * D],
            in_=wbf[k * 128:(k + 1) * 128, :])),
            deps=((p1_done + [prep_w_done[0]]) if k < 2 else
                  (prep_last if k >= DB - 2 else ())))
        wq_loads.append(tk)
    q_evac = []
    q_ev_par = [bvbo_done[-1], bvbo_done[-1]]
    for cc in range(DB):
        par = cc % 2
        def mmq(e, cc=cc, par=par):
            ii = None
            for k in range(DB):
                for hf in range(NHF):
                    ii = e.matmul(
                        PS[:, (par * 2 + hf) * 512:(par * 2 + hf) * 512 + WQ],
                        Wsb[:, k * D + cc * 128: k * D + cc * 128 + 128],
                        XT_q[:, k * TQ + hf * 512: k * TQ + hf * 512 + WQ],
                        start=(k == 0), stop=(k == DB - 1))
            return ii
        tq = P.op("pe", mmq, deps=[wq_loads[-1], wq_loads[-2], xt_q_done, q_ev_par[par]])
        def evq(e, cc=cc, par=par):
            return e.activation(Qbig[:, cc * TQ:(cc + 1) * TQ],
                                PS[:, par * 1024: par * 1024 + TQ],
                                AF.Identity, bias=bq_t[:, cc:cc + 1])
        te = P.op("act", evq, deps=[tq, t_bq])
        q_ev_par[par] = te
        q_evac.append(te)
    qbig_done = q_evac[-1]
    qbig_done2 = q_evac[-2]

    class Feeder:

        N_GU = 2 * (2 * FFH // 128)
        N_WP = 2 * DB * (FFH // D)

        def __init__(self):
            self.n = self.N_GU + self.N_WP
            self.ia = 0
            self.ib = 0
            self.load_tok = {}
            self.buf_free = [[qbig_done] if i % 2 == 0 else [qbig_done2]
                             for i in range(5)]
            self.chain = {0: [qbig_done], 1: [qbig_done2]}
            self.qpt_st = [qbig_done, qbig_done2, qbig_done]
            self.t5_tok = {}
            self.last = {}

        def _load(self, i):
            par = i % 2
            sl = i % 5
            if i < self.N_GU:
                rb, hf = i // 2, i % 2
                srcap = wgu_d[rb * 128:(rb + 1) * 128,
                              hf * HT:(hf + 1) * HT]
            else:
                j = i - self.N_GU
                t, sub, hf = j // 8, (j % 8) // 2, j % 2
                srcap = wp_d[t * 128:(t + 1) * 128,
                             sub * D + hf * HT:sub * D + (hf + 1) * HT]
            self.load_tok[i] = P.dma(
                "pool", "dga%d" % par,
                (lambda e, srcap=srcap, sl=sl:
                 e.dma_start(out=qb[sl][:, :], in_=srcap)),
                deps=self.buf_free[sl])
            self.last["dga%d" % par] = self.load_tok[i]

        def _stage_a(self, i):
            par = i % 2
            sl = i % 5
            qs = i % 3
            b = qb[sl]
            ng = HT // 64
            for j in (i, i + 1, i + 2):
                if j < self.n and j not in self.load_tok:
                    self._load(j)
            t1 = P.op("dve", (lambda e, b=b, par=par: e.tensor_reduce(
                qgs[par][:, 0:ng],
                b[:, :].rearrange("p (g s) -> p g s", s=64),
                AX.X, AL.add, apply_absolute_value=True)),
                deps=[self.load_tok[i]] + self.chain[par])
            t23 = P.op("dve", (lambda e, par=par: e.tensor_scalar(
                qgc[par][:, 0:ng], qgs[par][:, 0:ng], 1.0 / 64, 1e-8,
                AL.mult, AL.max)), deps=[t1])
            t4 = P.op("dve", (lambda e, par=par: e.reciprocal(
                qgr[par][:, 0:ng], qgc[par][:, 0:ng])),
                deps=[t23, self.t5_tok.get(i - 2)])
            self.t5_tok[i] = P.op(
                "pool", (lambda e, b=b, par=par, qs=qs: e.tensor_tensor(
                    qquot[qs][:, :].rearrange("p (g s) -> p g s", s=64),
                    b[:, :].rearrange("p (g s) -> p g s", s=64),
                    qgr[par][:, 0:ng].unsqueeze(2).broadcast_to(
                        [128, ng, 64]),
                    AL.mult)), deps=[t4, self.qpt_st[qs]], sem="pl")

        def _stage_b(self, i):
            par = i % 2
            sl = i % 5
            qs = i % 3
            b = qb[sl]
            ng = HT // 64
            t6 = P.op("pool", (lambda e, b=b, qs=qs: e.tensor_scalar(
                b[:, :], qquot[qs][:, :], -0.5, None, AL.is_lt)),
                deps=[self.t5_tok[i]], sem="pl")
            t7a = P.op("dve", (lambda e, qs=qs: e.tensor_scalar(
                qquot[qs][:, :], qquot[qs][:, :], 0.5, None, AL.is_gt)),
                deps=[t6])
            t7 = P.op("dve", (lambda e, b=b, qs=qs: e.tensor_tensor(
                b[:, :], qquot[qs][:, :], b[:, :], AL.subtract)),
                deps=[t7a])
            t8 = P.op("dve", (lambda e, b=b, par=par: e.tensor_tensor(
                b[:, :].rearrange("p (g s) -> p g s", s=64),
                b[:, :].rearrange("p (g s) -> p g s", s=64),
                qgc[par][:, 0:ng].unsqueeze(2).broadcast_to([128, ng, 64]),
                AL.mult)), deps=[t7])
            self.chain[par] = [t8]
            tq_eng, tq_sem = ("act", "dqt%d" % par)
            tp = P.dma(tq_eng, tq_sem,
                       (lambda e, b=b, qs=qs: e.dma_start(
                           out=qpt[qs][:, :].rearrange(
                               "p (k r) -> p k r", k=DB // 2),
                           in_=b[:, :], transpose=True)),
                       deps=[t8, self.qpt_st[qs]])
            self.last[tq_sem] = tp
            self.buf_free[sl] = [tp]
            if i < self.N_GU:
                rb, hf = i // 2, i % 2
                def stg(e, rb=rb, hf=hf, qs=qs):
                    return e.dma_start(
                        out=wgqt[rb * 128:(rb + 1) * 128,
                                 hf * HT:(hf + 1) * HT],
                        in_=qpt[qs][:, :])
                ts_ = P.dma("sync", "dqs%d" % par, stg, deps=[tp])
            else:
                j = i - self.N_GU
                t, sub, hf = j // 8, (j % 8) // 2, j % 2
                def stw(e, t=t, sub=sub, hf=hf, qs=qs):
                    dst = wptdram[sub * D + hf * HT:
                                  sub * D + (hf + 1) * HT,
                                  t * 128:(t + 1) * 128]
                    dst = dst.rearrange("(k p) r -> p k r", p=128)
                    return e.dma_start(
                        out=dst, in_=qpt[qs][:, :].rearrange(
                            "p (k r) -> p k r", k=DB // 2))
                ts_ = P.dma("sync", "dqs%d" % par, stw, deps=[tp])
            self.qpt_st[qs] = ts_
            self.last["dqs%d" % par] = ts_
            if i + 2 < self.n and (i + 2) not in self.load_tok:
                self._load(i + 2)

        def step(self, k=1):
            for _ in range(k):
                if self.ia < self.n:
                    self._stage_a(self.ia)
                    self.ia += 1
                if self.ib < self.ia - 1:
                    self._stage_b(self.ib)
                    self.ib += 1

        def finish(self):
            while self.ia < self.n:
                self._stage_a(self.ia)
                self.ia += 1
            while self.ib < self.n:
                self._stage_b(self.ib)
                self.ib += 1
            return [t for t in self.last.values()] + \
                self.chain[0] + self.chain[1]

    feeder = Feeder()

    wk_loads = []
    for k in range(DB):
        tk = P.dma("sync", "dw%d" % (k % 2), (lambda e, k=k: e.dma_start(
            out=Wsb[:, k * D:(k + 1) * D],
            in_=wbf[(DB + k) * 128:(DB + k + 1) * 128, :])),
            deps=([qbig_done, qbig_done2, prep_w_done[1]] if k < 2 else ()))
        wk_loads.append(tk)
    SW = min(512, TQ)
    NST = T // SW
    k_stores = []
    kst_par = [None, None]
    kev_par = [None, None]
    for h in range(NH):
        for st in range(NST):
            par = (h * NST + st) % 2
            own = (st * SW < TQ)
            xt = XT_own if own else XT_oth
            base = st * SW - (0 if own else TQ)
            def mmk(e, h=h, par=par, xt=xt, base=base):
                ii = None
                for k in range(DB):
                    ii = e.matmul(
                        PS[:, par * 512:par * 512 + SW],
                        Wsb[:, k * D + h * 128: k * D + h * 128 + 128],
                        xt[:, k * TQ + base: k * TQ + base + SW],
                        start=(k == 0), stop=(k == DB - 1))
                return ii
            tk = P.op("pe", mmk,
                      deps=[wk_loads[-1], wk_loads[-2],
                            xt_own_done if own else xt_oth_done,
                            kev_par[par]])
            def evk(e, h=h, par=par):
                return e.activation(kvst[par][:, 0:SW],
                                    PS[:, par * 512:par * 512 + SW],
                                    AF.Identity, bias=bk_t[:, h:h + 1])
            te = P.op("act", evk, deps=[tk, t_bk, kst_par[par]])
            kev_par[par] = te
            def stk(e, h=h, st=st, par=par):
                return e.dma_start(
                    out=kdram[:, h * T + st * SW: h * T + (st + 1) * SW],
                    in_=kvst[par][:, 0:SW])
            ts_ = P.dma("sync", "dks%d" % par, stk, deps=[te])
            kst_par[par] = ts_
            k_stores.append(ts_)
            feeder.step(3)

    wv_loads = []
    for k in range(DB):
        tk = P.dma("sync", "dw%d" % (k % 2), (lambda e, k=k: e.dma_start(
            out=Wsb[:, k * D:(k + 1) * D],
            in_=wbf[(2 * DB + k) * 128:(2 * DB + k + 1) * 128, :])),
            deps=([k_stores[-1], k_stores[-2], prep_w_done[2]]
                  if k < 2 else ()))
        wv_loads.append(tk)
    v_stores = []
    vst_par = [None, None]
    vev_par = [None, None]
    for sc in range(NT):
        own = sc < NCH
        xt = XT_own if own else XT_oth
        scc = sc % NCH
        for ch in range(2):
            par = (sc * 2 + ch) % 2
            def mmv(e, xt=xt, scc=scc, ch=ch, par=par):
                ii = None
                for k in range(DB):
                    for q in range(2):
                        ii = e.matmul(
                            PS[:, (par * 2 + q) * 512:(par * 2 + q + 1) * 512],
                            xt[:, k * TQ + scc * 128: k * TQ + scc * 128 + 128],
                            Wsb[:, k * D + ch * 1024 + q * 512:
                                k * D + ch * 1024 + (q + 1) * 512],
                            start=(k == 0), stop=(k == DB - 1))
                return ii
            tv = P.op("pe", mmv,
                      deps=[wv_loads[-1], wv_loads[-2],
                            xt_own_done if own else xt_oth_done,
                            vev_par[par]])
            def evv(e, ch=ch, par=par):
                return e.tensor_tensor(kvst[par][:, 0:1024],
                                       PS[:, par * 1024: par * 1024 + 1024],
                                       bv_e[:, ch * 1024:(ch + 1) * 1024],
                                       AL.add)
            te = P.op("dve", evv, deps=[tv, vst_par[par]] + bvbo_done)
            vev_par[par] = te
            def stv(e, sc=sc, ch=ch, par=par):
                dst = vdram[ch * 8:(ch + 1) * 8, :,
                            sc * 128:(sc + 1) * 128].transpose([1, 0, 2])
                src = kvst[par][:, 0:1024].rearrange("p (h d) -> p h d", h=8)
                return e.dma_start(out=dst, in_=src)
            ts_ = P.dma("sync", "dvs%d" % par, stv, deps=[te])
            vst_par[par] = ts_
            v_stores.append(ts_)
            feeder.step(3)
    kv_dep = v_stores[-1]
    kv_dep2 = v_stores[-2]

    mload = P.dma("sync", "dc",
                  lambda e: e.dma_start(
                      out=mask_sb[:, :].rearrange("p (c q) -> p c q", c=NCH),
                      in_=mask_d.transpose([1, 0, 2])),
                  deps=[qbig_done, qbig_done2])

    kh_load, vh_load = {}, {}
    kh_free = [None, None]
    vh_free = [None, None]
    pt_free = [None, None]
    l_free = [None, None]
    av_free = None
    rs_free = None
    obig_toks = []

    def load_head(h):
        par = h % 2
        kh_load[h] = P.dma("sync", "dkh%d" % par,
                           (lambda e, h=h, par=par: e.dma_start(
                               out=Kh[par][:, :],
                               in_=kdram[:, h * T:(h + 1) * T])),
                           deps=[kv_dep, kv_dep2, kh_free[par]])
        vh_load[h] = P.dma("sync", "dvh%d" % par,
                           (lambda e, h=h, par=par: e.dma_start(
                               out=Vh[par][:, :], in_=vdram[h, :, :])),
                           deps=[kv_dep, kv_dep2, vh_free[par]])

    load_head(0)
    load_head(1)
    for h in range(NH):
        hp = h % 2
        d_last = None
        last_exp = None
        def emit_logits(gsb):
            ps = gsb // NCH
            jc = gsb % NCH
            lpar = gsb % 2
            def mml(e, h=h, gsb=gsb, jc=jc, ps=ps, lpar=lpar):
                qv = Qbig[:, :].rearrange("p (c q) -> p c q", c=DB)
                ii = None
                for rh in range(2):
                    if ps == 0:
                        e.matmul(
                            PS[:, lpar * 1024 + rh * 8 * RR:
                               lpar * 1024 + (rh + 1) * 8 * RR],
                            ident_b[:, :],
                            mask_sb[:, jc * TQ + rh * 8 * RR:
                                    jc * TQ + (rh + 1) * 8 * RR],
                            start=True, stop=False)
                    ii = e.matmul(
                        PS[:, lpar * 1024 + rh * 8 * RR:
                           lpar * 1024 + (rh + 1) * 8 * RR],
                        Kh[h % 2][:, gsb * 128:(gsb + 1) * 128],
                        qv[:, rh * 8:(rh + 1) * 8, h * RR:(h + 1) * RR],
                        start=(ps != 0), stop=True)
                return ii
            return P.op("pe", mml,
                        deps=[kh_load[h], l_free[lpar], qbig_done, t_idb,
                              mload])

        def emit_exp(gsb, tl):
            ps = gsb // NCH
            lpar = gsb % 2
            ppar = gsb % 2
            def exp_(e, ps=ps, lpar=lpar, ppar=ppar):
                b = sh1[:, 0:1] if ps == 0 else p2b[:, 0:1]
                return e.activation(Pt[ppar][:, :],
                                    PS[:, lpar * 1024: lpar * 1024 + TQ],
                                    AF.Exp, bias=b, scale=SCL)
            return P.op("act", exp_, deps=[tl, pt_free[ppar], t_p2, t_sh1])

        def emit_av(gsb, texp):
            ppar = gsb % 2
            def mmav(e, h=h, gsb=gsb, ppar=ppar):
                st = (gsb == 0)
                sp = (gsb == 2 * NCH - 1)
                ii = None
                for rh in range(NHF):
                    e.matmul(PS[0:1, 3072 + rh * 512:3072 + rh * 512 + WQ],
                             onc[:, :],
                             Pt[ppar][:, rh * 512:rh * 512 + WQ],
                             start=st, stop=sp)
                for rh in range(NHF):
                    ii = e.matmul(
                        PS[:, 2048 + rh * 512:2048 + rh * 512 + WQ],
                        Vh[h % 2][:, gsb * 128:(gsb + 1) * 128],
                        Pt[ppar][:, rh * 512:rh * 512 + WQ],
                        start=st, stop=sp)
                return ii
            return P.op("pe", mmav,
                        deps=[texp, vh_load[h], t_onc] +
                        ([av_free, rs_free] if gsb == 0 else []))

        texp_q = {}
        tl0 = emit_logits(0)
        texp_q[0] = emit_exp(0, tl0)
        l_free[0] = texp_q[0]
        for gsb in range(1, 2 * NCH):
            lpar = gsb % 2
            tl = emit_logits(gsb)
            texp_q[gsb] = emit_exp(gsb, tl)
            l_free[lpar] = texp_q[gsb]
            tav = emit_av(gsb - 1, texp_q[gsb - 1])
            pt_free[(gsb - 1) % 2] = tav
            d_last = tav
            if gsb % 2 == 0 and gsb <= 12:
                feeder.step(1)
        tav = emit_av(2 * NCH - 1, texp_q[2 * NCH - 1])
        pt_free[(2 * NCH - 1) % 2] = tav
        d_last = tav
        last_exp = texp_q[2 * NCH - 1]
        kh_free[hp] = d_last
        vh_free[hp] = d_last
        if h + 2 < NH:
            load_head(h + 2)
        tre = P.op("dve", (lambda e: e.reciprocal(
            rs_rcp[0:1, :], PS[0:1, 3072:3072 + TQ])), deps=[d_last])
        rs_free = tre
        def mmx(e):
            ii = None
            for rh in range(NHF):
                ii = e.matmul(PS[:, rh * 512:rh * 512 + WQ], onr[0:1, :],
                              rs_rcp[0:1, rh * 512:rh * 512 + WQ],
                              start=True, stop=True)
            return ii
        txp = P.op("pe", mmx,
                   deps=[tre, last_exp, l_free[0], l_free[1], t_onr])
        tcp = P.op("act", (lambda e: e.activation(
            rc_sb[:, :], PS[:, 0:TQ], AF.Copy)), deps=[txp])
        l_free[0] = tcp
        l_free[1] = tcp
        def evo(e, h=h):
            return e.tensor_tensor(Obig[:, h * TQ:(h + 1) * TQ],
                                   PS[:, 2048:2048 + TQ], rc_sb[:, :],
                                   AL.mult)
        tob = P.op("dve", evo, deps=[tcp, d_last])
        av_free = tob
        obig_toks.append(tob)
    p5_done = obig_toks[-1]
    feeder_done = feeder.finish()

    wo_loads = []
    for k in range(NH):
        tk = P.dma("sync", "dw%d" % (k % 2), (lambda e, k=k: e.dma_start(
            out=Wsb[:, k * D:(k + 1) * D],
            in_=wbf[(3 * DB + k) * 128:(3 * DB + k + 1) * 128, :])),
            deps=([p5_done, prep_w_done[3]] if k < 2 else ()))
        wo_loads.append(tk)
    acc_stores = []
    ac_par = [None, None]
    ae_par = [None, None]
    for rc in range(NCH):
        par = rc % 2
        def mmo(e, rc=rc, par=par):
            ii = None
            for dt in range(4):
                for h in range(NH):
                    ii = e.matmul(
                        PS[:, par * 2048 + dt * 512:
                           par * 2048 + (dt + 1) * 512],
                        Obig[:, h * TQ + rc * 128: h * TQ + rc * 128 + 128],
                        Wsb[:, h * D + dt * 512: h * D + (dt + 1) * 512],
                        start=(h == 0), stop=(h == NH - 1))
            return ii
        to = P.op("pe", mmo, deps=[wo_loads[-1], wo_loads[-2], p5_done, ae_par[par]])
        def eva(e, par=par):
            return e.tensor_tensor(accst[par][:, :],
                                   PS[:, par * 2048: par * 2048 + D],
                                   bo_e[:, :], AL.add)
        te = P.op("dve", eva, deps=[to, ac_par[par]] + bvbo_done +
                  (feeder_done if rc < 2 else []))
        ae_par[par] = te
        def sta(e, rc=rc, par=par):
            return e.dma_start(out=accdram[rc * 128:(rc + 1) * 128, :],
                               in_=accst[par][:, :])
        ts_ = P.dma("sync", "das%d" % par, sta, deps=[te])
        ac_par[par] = ts_
        acc_stores.append(ts_)
    p6_done = acc_stores[-1]
    p6_done2 = acc_stores[-2] if NCH > 1 else acc_stores[-1]

    wT_free = [[p6_done]] * 4
    gate_free = [[p6_done], [p6_done]]
    h_prev = [p6_done]
    wpq_done = feeder_done
    ps_free = [p6_done]
    out_toks = []
    acct_free = [None, None]
    xtf_free = [None, None]
    ost_free = [None, None]
    inv_tp_free = None

    for sh in range(NSHALF):
        srow0 = sh * NCH_H
        scol0 = sh * SHALF
        if sh > 0:
            gate_free = [list(ps_free), list(ps_free)]
            h_prev = list(ps_free)
            wT_free = [[wpT4_free[0]], [wpT4_free[1]],
                       [wpT4_free[2]], [wpT4_free[3]]]
        h2_prev = None
        hbig_toks = []
        hsum_last = None
        for j in range(NPAIR):
            par = j % 2
            tpg = P.dma("pool", "dg%d" % (par * 2),
                        (lambda e, j=j, par=par: e.dma_start(
                            out=wT[par * 2][:, :],
                            in_=wgqt[j * 128:(j + 1) * 128, :])),
                        deps=wT_free[par * 2] +
                        (feeder_done if sh == 0 and j < 2 else []))
            tpu = P.dma("pool", "dg%d" % (par * 2 + 1),
                        (lambda e, j=j, par=par: e.dma_start(
                            out=wT[par * 2 + 1][:, :],
                            in_=wgqt[FFH + j * 128:
                                     FFH + (j + 1) * 128, :])),
                        deps=wT_free[par * 2 + 1] +
                        (feeder_done if sh == 0 and j < 2 else []))
            gb = 0 if par == 0 else 4
            def mmgu(e, par=par, gb=gb, scol0=scol0):
                ii = None
                for k in range(DB):
                    for rh in range(max(1, SHALF // 512)):
                        w = min(512, SHALF)
                        e.matmul(PS[:, (gb + rh) * 512:(gb + rh) * 512 + w],
                                 wT[par * 2][:, k * 128:(k + 1) * 128],
                                 XT_own[:, k * TQ + scol0 + rh * 512:
                                        k * TQ + scol0 + rh * 512 + w],
                                 start=(k == 0), stop=(k == DB - 1))
                        ii = e.matmul(
                            PS[:, (2 + rh) * 512:(2 + rh) * 512 + w],
                            wT[par * 2 + 1][:, k * 128:(k + 1) * 128],
                            XT_own[:, k * TQ + scol0 + rh * 512:
                                   k * TQ + scol0 + rh * 512 + w],
                            start=(k == 0), stop=(k == DB - 1))
                return ii
            tmm = P.op("pe", mmgu,
                       deps=[tpg, tpu, xt_own_done] + gate_free[par] + h_prev)
            wT_free[par * 2] = [tmm]
            wT_free[par * 2 + 1] = [tmm]
            tsi = P.op("act", (lambda e, gb=gb: e.activation(
                silu_sb[:, :], PS[:, gb * 512: gb * 512 + SHALF], AF.Silu)),
                deps=[tmm])
            gate_free[par] = [tsi]
            th = P.op("dve", (lambda e, j=j: e.tensor_tensor(
                Hbig[:, j * SHALF:(j + 1) * SHALF], silu_sb[:, :],
                PS[:, 1024:1024 + SHALF], AL.mult)), deps=[tsi])
            h_prev = [th]
            hbig_toks.append(th)
            th2 = P.op("dve", (lambda e, j=j: e.tensor_tensor(
                h2_sb[:, :], Hbig[:, j * SHALF:(j + 1) * SHALF],
                Hbig[:, j * SHALF:(j + 1) * SHALF], AL.mult)),
                deps=[th, h2_prev])
            def mmh2(e, j=j):
                ii = None
                for rh in range(max(1, SHALF // 512)):
                    w = min(512, SHALF)
                    ii = e.matmul(
                        PS[0:1, 3072 + rh * 512: 3072 + rh * 512 + w],
                        onc[:, :], h2_sb[:, rh * 512:rh * 512 + w],
                        start=(j == 0), stop=(j == NPAIR - 1))
                return ii
            th3 = P.op("pe", mmh2, deps=[th2, t_onc])
            h2_prev = th3
            hsum_last = th3

        tr1 = P.op("dve", (lambda e: e.tensor_scalar(
            r1row[0:1, :], PS[0:1, 3072:3072 + SHALF], 1.0 / FFH, EPS_RMS,
            AL.mult, AL.add)), deps=[hsum_last])
        tr2 = P.op("act", (lambda e: e.activation(
            r1row[0:1, :], r1row[0:1, :], AF.Sqrt)), deps=[tr1])
        tr3 = P.op("dve", (lambda e: e.reciprocal(
            r1row[0:1, :], r1row[0:1, :])), deps=[tr2])
        tms = P.op("dve", (lambda e: e.memset(trin[:, :], 0.0)),
                   deps=[inv_tp_free])
        inv_prev = None
        psT_free = hsum_last
        for c in range(NCH_H):
            tcc = P.op("dve", (lambda e, c=c: e.tensor_copy(
                trin[0:1, :], r1row[0:1, c * 128:(c + 1) * 128])),
                deps=[tr3, tms, inv_prev])
            tt = P.op("pe", (lambda e: e.transpose(
                PS[:, 0:128], trin[:, :], ident[:, :])),
                deps=[tcc, t_id, psT_free])
            tc2 = P.op("dve", (lambda e, c=c, srow0=srow0: e.tensor_copy(
                invcol[:, srow0 + c:srow0 + c + 1], PS[:, 0:1])), deps=[tt])
            psT_free = tc2
            inv_prev = tt
        inv_done = psT_free
        inv_tp_free = inv_done

        ps_free = [inv_done] + gate_free[0] + gate_free[1] + h_prev
        wpT4_free = [wT_free[i][-1] for i in range(4)]
        for sg in range(NSG):
            last_mm = None
            for k in range(NPAIR):
                wpar = k % 4
                tw = P.dma("act" if k % 2 == 0 else "pool", "dt%d" % wpar,
                           (lambda e, k=k, wpar=wpar: e.dma_start(
                               out=wpT4[wpar][:, :],
                               in_=wptdram[k * 128:(k + 1) * 128, :])),
                           deps=[wpT4_free[wpar]] +
                           (wpq_done if sh == 0 and sg == 0 and k < 4
                            else []))
                def mmp(e, sg=sg, k=k, wpar=wpar):
                    ii = None
                    for sc in range(SCP):
                        row = sg * SCP + sc
                        for dt in range(4):
                            ii = e.matmul(
                                PS[:, sc * 2048 + dt * 512:
                                   sc * 2048 + (dt + 1) * 512],
                                Hbig[:, k * SHALF + row * 128:
                                     k * SHALF + row * 128 + 128],
                                wpT4[wpar][:, dt * 512:(dt + 1) * 512],
                                start=(k == 0), stop=(k == NPAIR - 1))
                    return ii
                tm = P.op("pe", mmp,
                          deps=[tw, hbig_toks[k]] +
                          (ps_free if k == 0 else []))
                wpT4_free[wpar] = tm
                last_mm = tm
            nps = []
            for sc in range(SCP):
                row = sg * SCP + sc
                grow = srow0 + row
                par = row % 2
                ta = P.dma("sync", "dal%d" % par,
                           (lambda e, grow=grow, par=par: e.dma_start(
                               out=acct[par][:, :],
                               in_=accdram[grow * 128:(grow + 1) * 128, :])),
                           deps=[p6_done, p6_done2, acct_free[par]])
                tx = P.dma("pool", "dxl%d" % par,
                           (lambda e, grow=grow, par=par: e.dma_start(
                               out=xt_[par][:, :],
                               in_=x_kv[grow * 128:(grow + 1) * 128, :])),
                           deps=[xtf_free[par]] +
                           ([p6_done, p6_done2] if sh == 0 else []))
                def ev1(e, sc=sc, grow=grow, par=par):
                    return e.scalar_tensor_tensor(
                        ost[par][:, :], PS[:, sc * 2048: sc * 2048 + D],
                        invcol[:, grow:grow + 1], acct[par][:, :],
                        AL.mult, AL.add)
                t1 = P.op("dve", ev1,
                          deps=[last_mm, ta, inv_done, ost_free[par]])
                acct_free[par] = t1
                def ev2(e, par=par):
                    return e.tensor_tensor(ost[par][:, :], ost[par][:, :],
                                           xt_[par][:, :], AL.add)
                t2 = P.op("dve", ev2, deps=[t1, tx])
                xtf_free[par] = t2
                def sto(e, grow=grow, par=par):
                    return e.dma_start(
                        out=out_d[grow * 128:(grow + 1) * 128, :],
                        in_=ost[par][:, :])
                ts_ = P.dma("sync", "do%d" % par, sto, deps=[t2])
                ost_free[par] = ts_
                out_toks.append(ts_)
                nps.append(t1)
            ps_free = nps

    with nc.Block() as block:
        sems = {}
        guards = []
        for s in P.sem_names:
            g = nc.semaphore(s)
            sems[s] = g.__enter__()
            guards.append(g)

        fin0 = P.counts.get("do0", 0)
        fin1 = P.counts.get("do1", 0)

        @block.tensor
        def _(t):
            P.run("pe", t, sems)

        @block.scalar
        def _(s):
            P.run("act", s, sems)

        @block.vector
        def _(v):
            P.run("dve", v, sems)

        @block.gpsimd
        def _(g):
            P.run("pool", g, sems)

        @block.sync
        def _(sy):
            P.run("sync", sy, sems)
            if fin0:
                sy.wait_ge(sems["do0"], fin0)
            if fin1:
                sy.wait_ge(sems["do1"], fin1)

    return nc



def host_prep(inputs, T):
    TQ = T // 2
    RR = T // 32
    NCH = TQ // 128
    x = np.ascontiguousarray(inputs["L0_attn_x"][:, :T, :])
    cc_i = np.arange(16)
    rr_i = np.arange(RR)
    sig = np.arange(TQ)
    s_loc = 16 * (sig % RR) + sig // RR
    mask1 = np.where(s_loc[:, None] <= s_loc[None, :], 0.0, NEG)
    mask1 = mask1.reshape(NCH, 128, TQ).astype(BF)
    ident = np.eye(128, dtype=np.float32)
    ones_col = np.ones((128, 1), dtype=BF)
    ones_row = np.ones((1, 128), dtype=np.float32)
    bq_t = np.ascontiguousarray(
        inputs["L0_attn_bq"].reshape(16, 128).T.astype(np.float32))
    bk_t = np.ascontiguousarray(
        inputs["L0_attn_bk"].reshape(16, 128).T.astype(np.float32))
    bv_r = np.ascontiguousarray(
        inputs["L0_attn_bv"].reshape(1, D).astype(np.float32))
    bo_r = np.ascontiguousarray(
        inputs["L0_attn_bo"].reshape(1, D).astype(np.float32))
    own_tok = {}
    for half in (0, 1):
        tok = (T // 2) * half + 16 * rr_i[None, :] + cc_i[:, None]
        own_tok[half] = tok.reshape(-1)
    maps = []
    for core in range(8):
        b, half = core // 2, core % 2
        own, oth = own_tok[half], own_tok[1 - half]
        x_kv = np.ascontiguousarray(x[b][np.concatenate([own, oth])])
        qrows = np.array([h * (T // 16) + RR * half + rr
                          for h in range(16) for rr in range(RR)])
        x_q = np.ascontiguousarray(x[b][qrows])
        p2b = np.full((128, 1), SHIFT + (NEG if half == 0 else 0.0),
                      dtype=np.float32)
        maps.append({
            "x_kv": x_kv, "x_q": x_q,
            "Wq": inputs["L0_attn_Wq"], "Wk": inputs["L0_attn_Wk"],
            "Wv": inputs["L0_attn_Wv"], "Wo": inputs["L0_attn_Wo"],
            "bq_t": bq_t, "bk_t": bk_t, "bv_r": bv_r, "bo_r": bo_r,
            "wgu": inputs["L0_ffn_w_gate_up"],
            "wp": inputs["L0_ffn_w_proj"],
            "mask1": mask1, "pass2b": p2b, "ident": ident,
            "ident_b": ident.astype(BF),
            "ones_col": ones_col, "ones_row": ones_row,
        })
    return maps, own_tok


def host_gather(results, own_tok, T, B=4):
    out = np.empty((B, T, D), dtype=np.float32)
    for core in range(8):
        b, half = core // 2, core % 2
        out[b][own_tok[half]] = results[core]["out"]
    return out


def kernel(**inputs):
    T = 2048
    nc = build_nc(T)
    in_maps, own_tok = host_prep(inputs, T)
    res = run_bass_kernel_spmd(nc, in_maps, core_ids=list(range(8)))
    return host_gather(res.results, own_tok, T)


# version 15
# speedup vs baseline: 2.0305x; 2.0305x over previous
import numpy as np
import ml_dtypes

import concourse.bass as bass
import concourse.mybir as mybir
from concourse.bass_utils import run_bass_kernel_spmd

BF = ml_dtypes.bfloat16
F32 = mybir.dt.float32
BF16 = mybir.dt.bfloat16
AF = mybir.ActivationFunctionType
AL = mybir.AluOpType
AX = mybir.AxisListType

D = 2048
NH = 16
DB = D // 128
FFH = 8192
NPAIR = FFH // 128
EPS_LN = 1e-5
EPS_RMS = float(np.finfo(np.float32).eps)
SCL = float(1.0 / np.float32(np.sqrt(np.float64(128))))
SHIFT = -12.0
NEG = -1.0e30


class Prog:

    def __init__(self):
        self.counts = {}
        self.ops = {k: [] for k in ("pe", "act", "dve", "pool", "sync")}

    def op(self, eng, emit, deps=(), sem=None, inc=1):
        if sem is None:
            sem = {"pe": "pe", "act": "act", "dve": "dve"}[eng]
        prev = self.counts.get(sem, 0)
        self.counts[sem] = prev + inc
        tok = (sem, self.counts[sem])
        self.ops[eng].append((emit, tuple(d for d in deps if d), sem, inc))
        return tok

    def dma(self, eng, sem, emit, deps=()):
        prev = self.counts.get(sem, 0)
        deps = list(deps) + ([(sem, prev)] if prev else [])
        return self.op(eng, emit, deps, sem=sem, inc=16)

    @property
    def sem_names(self):
        return list(self.counts.keys())

    def run(self, engname, eng, sems):
        waited = {}
        for emit, deps, sem, inc in self.ops[engname]:
            for (s, c) in deps:
                if waited.get(s, 0) < c:
                    eng.wait_ge(sems[s], c)
                    waited[s] = c
            emit(eng).then_inc(sems[sem], inc)


def build_nc(T, dbg=False):
    assert T % 256 == 0
    TQ = T // 2
    RR = T // 32
    NCH = TQ // 128
    NT = T // 128
    NHF = max(1, TQ // 512)
    WQ = min(512, TQ)
    NSHALF = 2 if NCH >= 2 else 1
    SHALF = TQ // NSHALF
    NCH_H = SHALF // 128
    SCP = 2 if NCH_H >= 2 else 1
    NSG = NCH_H // SCP

    nc = bass.Bass("TRN2", target_bir_lowering=False, debug=False)

    x_kv = nc.dram_tensor("x_kv", [T, D], F32, kind="ExternalInput").ap()
    x_q = nc.dram_tensor("x_q", [TQ, D], F32, kind="ExternalInput").ap()
    Wq_d = nc.dram_tensor("Wq", [D, D], F32, kind="ExternalInput").ap()
    Wk_d = nc.dram_tensor("Wk", [D, D], F32, kind="ExternalInput").ap()
    Wv_d = nc.dram_tensor("Wv", [D, D], F32, kind="ExternalInput").ap()
    Wo_d = nc.dram_tensor("Wo", [D, D], F32, kind="ExternalInput").ap()
    bq_d = nc.dram_tensor("bq_t", [128, DB], F32, kind="ExternalInput").ap()
    bk_d = nc.dram_tensor("bk_t", [128, NH], F32, kind="ExternalInput").ap()
    bv_d = nc.dram_tensor("bv_r", [1, D], F32, kind="ExternalInput").ap()
    bo_d = nc.dram_tensor("bo_r", [1, D], F32, kind="ExternalInput").ap()
    wgu_d = nc.dram_tensor("wgu", [2 * FFH, D], F32, kind="ExternalInput").ap()
    wp_d = nc.dram_tensor("wp", [D, FFH], F32, kind="ExternalInput").ap()
    mask_d = nc.dram_tensor("mask1", [NCH, 128, TQ], BF16,
                            kind="ExternalInput").ap()
    p2b_d = nc.dram_tensor("pass2b", [128, 1], F32, kind="ExternalInput").ap()
    id_d = nc.dram_tensor("ident", [128, 128], F32, kind="ExternalInput").ap()
    onc_d = nc.dram_tensor("ones_col", [128, 1], BF16,
                           kind="ExternalInput").ap()
    idb_d = nc.dram_tensor("ident_b", [128, 128], BF16,
                           kind="ExternalInput").ap()
    onr_d = nc.dram_tensor("ones_row", [1, 128], F32,
                           kind="ExternalInput").ap()
    out_d = nc.dram_tensor("out", [TQ, D], F32, kind="ExternalOutput").ap()
    kd_ = "ExternalOutput" if dbg else "Internal"
    wbf = nc.dram_tensor("wbf", [4 * D, D], BF16, kind=kd_).ap()
    kdram = nc.dram_tensor("kdram", [128, NH * T], BF16, kind=kd_).ap()
    vdram = nc.dram_tensor("vdram", [NH, 128, T], BF16, kind=kd_).ap()
    accdram = nc.dram_tensor("accdram", [TQ, D], F32, kind=kd_).ap()
    wptdram = nc.dram_tensor("wptdram", [FFH, D], BF16, kind=kd_).ap()
    wgqt = nc.dram_tensor("wgqt", [2 * FFH, D], BF16, kind=kd_).ap()

    SB = nc.SBUF_PARTITION_SIZE_BYTES
    BASE = 16384 + 512

    def nb(shape, dt):
        return int(np.prod(shape[1:])) * (2 if dt == BF16 else 4)

    cur = [BASE]

    def at(name, shape, dt, o=None):
        if o is None:
            o = cur[0]
            cur[0] = o + ((nb(shape, dt) + 31) // 32) * 32
        assert o >= BASE and o + nb(shape, dt) <= SB, (name, o, nb(shape, dt))
        return nc.alloc_sbuf_tensor_at(name, list(shape), dt, offset=o)

    ident = at("ident", [128, 128], F32)
    ident_b = at("ident_b", [128, 128], BF16)
    bq_t = at("bq_t", [128, DB], F32)
    bk_t = at("bk_t", [128, NH], F32)
    p2b = at("p2b", [128, 1], F32)
    onc = at("onc", [128, 1], BF16)
    onr = at("onr", [1, 128], F32)
    bv_e = at("bv_e", [128, D], BF16)
    bo_e = at("bo_e", [128, D], BF16)
    invcol = at("invcol", [128, max(NCH, 8)], F32)
    sh1 = at("sh1", [128, 1], F32)
    C_END = cur[0]
    XT_own = at("XT_own", [128, DB * TQ], BF16)
    A_END = cur[0]
    Qbig = at("Qbig", [128, DB * TQ], BF16)
    E_END = cur[0]
    XT_oth = at("XT_oth", [128, DB * TQ], BF16)
    Obig = at("Obig", [128, NH * TQ], BF16, o=E_END)
    B_END = cur[0]
    XT_q = at("XT_q", [128, DB * TQ], BF16)
    Dm0 = B_END
    mask_sb = at("mask_sb", [128, NCH * TQ], BF16, o=Dm0)
    MSK_END = Dm0 + NCH * TQ * 2
    D_END = cur[0]
    W0 = cur[0]
    Wsb = at("Wsb", [128, DB * D], BF16)
    W_END = cur[0]
    assert W_END <= SB, ("W", W_END, SB)
    o = W0
    Xf = [at("Xf%d" % i, [128, D], F32, o=o + i * D * 4) for i in range(2)]
    o += 2 * D * 4
    Xc = at("Xc", [128, D], F32, o=o); o += D * 4
    XN = [at("XN%d" % i, [128, D], BF16, o=o + i * D * 2) for i in range(2)]
    o += 2 * D * 2
    ttrj = at("ttrj", [128, D], BF16, o=o); o += D * 2
    p1c = at("p1c", [128, 8], F32, o=o); o += 32
    assert o <= W_END
    o = W0
    Kh = [at("Kh%d" % i, [128, T], BF16, o=o + i * T * 2) for i in range(2)]
    o += 2 * T * 2
    Vh = [at("Vh%d" % i, [128, T], BF16, o=o + i * T * 2) for i in range(2)]
    o += 2 * T * 2
    Pt = [at("Pt%d" % i, [128, TQ], BF16, o=o + i * TQ * 2) for i in range(2)]
    o += 2 * TQ * 2
    rs_rcp = at("rs_rcp", [1, TQ], F32, o=o); o += TQ * 4
    rc_sb = at("rc_sb", [128, TQ], F32, o=o); o += TQ * 4
    assert o <= W_END
    hb = NPAIR * SHALF * 2
    HB0 = A_END if A_END + hb <= Dm0 else W_END
    assert HB0 + hb <= SB, (HB0, hb)
    Hbig = at("Hbig", [128, NPAIR * SHALF], BF16, o=HB0)
    HB_END = HB0 + hb
    KA0 = MSK_END if MSK_END + 2 * D * 4 <= W0 else max(W_END, HB_END)
    assert KA0 + 2 * D * 4 <= SB, (KA0,)
    TOP0 = W_END
    kvst = [at("kvst%d" % i, [128, 1024], BF16, o=TOP0 + i * 1024 * 2)
            for i in range(2)]
    accst = [at("accst%d" % i, [128, D], F32, o=KA0 + i * D * 4)
             for i in range(2)]
    HT = D // 2
    qb = [at("qb%d" % i, [128, HT], BF16, o=KA0 + i * HT * 2)
          for i in range(5)]
    qquot = [at("qquot%d" % i, [128, HT], BF16, o=KA0 + (5 + i) * HT * 2)
             for i in range(3)]
    qpt = qquot
    assert KA0 + 8 * HT * 2 <= W0
    QG0 = TOP0 + 2 * 1024 * 2
    qgs = [at("qgs%d" % i, [128, HT // 64], F32, o=QG0 + i * 64)
           for i in range(2)]
    qgc = [at("qgc%d" % i, [128, HT // 64], BF16, o=QG0 + 128 + i * 32)
           for i in range(2)]
    qgr = [at("qgr%d" % i, [128, HT // 64], F32, o=QG0 + 192 + i * 64)
           for i in range(2)]
    assert QG0 + 320 <= SB
    wcst = [at("wcst%d" % i, [128, D], BF16, o=W0 + 14 * D * 2 + i * D * 2)
            for i in range(2)]
    o = W0
    wnat = [at("wnat%d" % i, [128, D], BF16, o=o + i * D * 2)
            for i in range(4)]
    WNAT0 = o
    o += 4 * D * 2
    quot = at("quot", [128, D], BF16, o=o); o += D * 2
    m2b = at("m2b", [128, D], BF16, o=o); o += D * 2
    gsum = at("gsum", [128, D // 64], F32, o=o); o += (D // 64) * 4
    gsc = at("gsc", [128, D // 64], BF16, o=o); o += (D // 64) * 4
    grc = at("grc", [128, D // 64], F32, o=o); o += (D // 64) * 4
    wT = [at("wT%d" % i, [128, D], BF16, o=o + i * D * 2) for i in range(4)]
    WT0 = o
    o += 4 * D * 2
    wpt = [at("wpt%d" % i, [128, D], BF16, o=o + i * D * 2) for i in range(2)]
    o += 2 * D * 2
    silu_sb = at("silu_sb", [128, SHALF], BF16, o=o); o += SHALF * 2
    h2_sb = at("h2_sb", [128, SHALF], BF16, o=o); o += SHALF * 2
    r1row = at("r1row", [1, SHALF], F32, o=o); o += SHALF * 4
    trin = at("trin", [128, 128], F32, o=o); o += 128 * 4
    assert o <= W_END, ("p7", o, W_END)
    B9 = Dm0 if Dm0 + 4 * D * 4 <= W0 else KA0 + 2 * D * 4
    assert B9 + 4 * D * 4 <= SB, (B9,)
    assert B9 >= W0 or B9 + 4 * D * 4 <= W0 or Dm0 + 4 * D * 4 <= W0
    acct = [at("acct%d" % i, [128, D], F32, o=B9 + i * D * 4)
            for i in range(2)]
    xt_ = [at("xt_%d" % i, [128, D], F32, o=B9 + (2 + i) * D * 4)
           for i in range(2)]
    ost = [at("ost%d" % i, [128, D], F32, o=WNAT0 + i * D * 4)
           for i in range(2)]
    wpT4 = [at("wpT%d" % i, [128, D], BF16, o=WT0 + i * D * 2)
            for i in range(4)]

    PS = nc.alloc_psum_tensor("PS", [128, 4096], F32)
    P = Prog()

    t_id = P.dma("sync", "dc", lambda e: e.dma_start(out=ident[:, :], in_=id_d))
    t_idb = P.dma("sync", "dc", lambda e: e.dma_start(out=ident_b[:, :], in_=idb_d))
    t_bq = P.dma("sync", "dc", lambda e: e.dma_start(out=bq_t[:, :], in_=bq_d))
    t_bk = P.dma("sync", "dc", lambda e: e.dma_start(out=bk_t[:, :], in_=bk_d))
    t_p2 = P.dma("sync", "dc", lambda e: e.dma_start(out=p2b[:, :], in_=p2b_d))
    t_onc = P.dma("sync", "dc", lambda e: e.dma_start(out=onc[:, :], in_=onc_d))
    t_onr = P.dma("sync", "dc", lambda e: e.dma_start(out=onr[:, :], in_=onr_d))
    t_bv = P.dma("sync", "dc", lambda e: e.dma_start(out=Xf[0][0:1, :], in_=bv_d))
    t_bo = P.dma("sync", "dc", lambda e: e.dma_start(out=Xf[1][0:1, :], in_=bo_d))
    t_sh1 = P.op("dve", lambda e: e.memset(sh1[:, :], SHIFT))

    bank_dep = [None] * 4
    for src_t, stg, buf in ((t_bv, Xf[0], bv_e), (t_bo, Xf[1], bo_e)):
        for i in range(D // 512):
            bk_i = i % 4
            def em(e, stg=stg, i=i, bk_i=bk_i):
                return e.matmul(PS[:, bk_i * 512:(bk_i + 1) * 512],
                                onr[0:1, :], stg[0:1, i * 512:(i + 1) * 512],
                                start=True, stop=True)
            tmk = P.op("pe", em, deps=[src_t, t_onr, bank_dep[bk_i]])
            def ev(e, buf=buf, i=i, bk_i=bk_i):
                return e.tensor_copy(buf[:, i * 512:(i + 1) * 512],
                                     PS[:, bk_i * 512:(bk_i + 1) * 512])
            bank_dep[bk_i] = P.op("dve", ev, deps=[tmk])
    bvbo_done = [t for t in bank_dep if t]

    prep_srcs = [Wq_d, Wk_d, Wv_d, Wo_d]
    prep_last = [None, None]
    prep_w_done = {}
    prep_state = {"i": 0}

    def prep_step(k=1):
        for _ in range(k):
            i = prep_state["i"]
            if i >= 4 * DB:
                return
            prep_state["i"] = i + 1
            w, t = i // DB, i % DB
            par = i % 2
            tl = P.dma("pool", "dwc%d" % par,
                       (lambda e, w=w, t=t, par=par: e.dma_start(
                           out=wcst[par][:, :],
                           in_=prep_srcs[w][t * 128:(t + 1) * 128, :])),
                       deps=[prep_last[par]])
            ts_ = P.dma("sync", "dws%d" % par,
                        (lambda e, i=i, par=par: e.dma_start(
                            out=wbf[i * 128:(i + 1) * 128, :],
                            in_=wcst[par][:, :])), deps=[tl])
            prep_last[par] = ts_
            prep_w_done[w] = ts_

    n_tiles = NT + NCH
    xbar_toks = []
    ln_dve_last = None
    buf_free = [None, None]
    xn_free = [None, None]

    for i in range(n_tiles):
        par = i % 2
        src = (x_kv[i * 128:(i + 1) * 128, :] if i < NT
               else x_q[(i - NT) * 128:(i - NT + 1) * 128, :])
        tl = P.dma("sync", "dx%d" % par,
                   (lambda e, src=src, par=par:
                    e.dma_start(out=Xf[par][:, :], in_=src)),
                   deps=[buf_free[par]] + (bvbo_done if i < 2 else []))
        t1 = P.op("dve", (lambda e, par=par: e.reduce_sum(
            p1c[:, 0:1], Xf[par][:, :], AX.X)), deps=[tl, ln_dve_last])
        t2 = P.op("dve", lambda e: e.tensor_scalar(
            p1c[:, 1:2], p1c[:, 0:1], 1.0 / D, None, AL.mult), deps=[t1])
        t3 = P.op("dve", (lambda e, par=par: e.tensor_scalar(
            Xc[:, :], Xf[par][:, :], p1c[:, 1:2], None, AL.subtract)),
            deps=[t2])
        buf_free[par] = t3
        t4 = P.op("act", lambda e: e.activation(
            ttrj[:, :], Xc[:, :], AF.Square, accum_out=p1c[:, 2:3]),
            deps=[t3])
        t5 = P.op("dve", lambda e: e.tensor_scalar(
            p1c[:, 3:4], p1c[:, 2:3], 1.0 / D, EPS_LN, AL.mult, AL.add),
            deps=[t4])
        t6 = P.op("act", lambda e: e.activation(
            p1c[:, 4:5], p1c[:, 3:4], AF.Sqrt), deps=[t5])
        t7 = P.op("dve", lambda e: e.reciprocal(
            p1c[:, 5:6], p1c[:, 4:5]), deps=[t6])
        t8 = P.op("dve", (lambda e, par=par: e.tensor_scalar(
            XN[par][:, :], Xc[:, :], p1c[:, 5:6], None, AL.mult)),
            deps=[t7, xn_free[par]])
        ln_dve_last = t8
        if i < NCH:
            dsti, col = XT_own, i * 128
        elif i < NT:
            dsti, col = XT_oth, (i - NCH) * 128
        else:
            dsti, col = XT_q, (i - NT) * 128
        def tp(e, par=par, dsti=dsti, col=col):
            dst = dsti[:, :].rearrange("p (k s) -> p k s", k=DB)[:, :,
                                                                col:col + 128]
            return e.dma_start(out=dst, in_=XN[par][:, :], transpose=True)
        tk = P.dma("act", "dt%d" % par, tp, deps=[t8])
        xn_free[par] = tk
        xbar_toks.append(tk)
        prep_step(3)

    while prep_state["i"] < 4 * DB:
        prep_step(1)

    xt_own_done = xbar_toks[NCH - 1]
    xt_oth_done = xbar_toks[NT - 1]
    xt_q_done = xbar_toks[NT + NCH - 1]
    p1_done = [xbar_toks[-1], xbar_toks[-2], ln_dve_last]

    wq_loads = []
    for k in range(DB):
        tk = P.dma("sync", "dw%d" % (k % 2), (lambda e, k=k: e.dma_start(
            out=Wsb[:, k * D:(k + 1) * D],
            in_=wbf[k * 128:(k + 1) * 128, :])),
            deps=((p1_done + [prep_w_done[0]]) if k < 2 else
                  (prep_last if k >= DB - 2 else ())))
        wq_loads.append(tk)
    q_evac = []
    q_ev_par = [bvbo_done[-1], bvbo_done[-1]]
    for cc in range(DB):
        par = cc % 2
        def mmq(e, cc=cc, par=par):
            ii = None
            for k in range(DB):
                for hf in range(NHF):
                    ii = e.matmul(
                        PS[:, (par * 2 + hf) * 512:(par * 2 + hf) * 512 + WQ],
                        Wsb[:, k * D + cc * 128: k * D + cc * 128 + 128],
                        XT_q[:, k * TQ + hf * 512: k * TQ + hf * 512 + WQ],
                        start=(k == 0), stop=(k == DB - 1))
            return ii
        tq = P.op("pe", mmq, deps=[wq_loads[-1], wq_loads[-2], xt_q_done, q_ev_par[par]])
        def evq(e, cc=cc, par=par):
            return e.activation(Qbig[:, cc * TQ:(cc + 1) * TQ],
                                PS[:, par * 1024: par * 1024 + TQ],
                                AF.Identity, bias=bq_t[:, cc:cc + 1])
        te = P.op("act", evq, deps=[tq, t_bq])
        q_ev_par[par] = te
        q_evac.append(te)
    qbig_done = q_evac[-1]
    qbig_done2 = q_evac[-2]

    class Feeder:

        N_GU = 2 * (2 * FFH // 128)
        N_WP = 2 * DB * (FFH // D)

        def __init__(self):
            self.n = self.N_GU + self.N_WP
            self.ia = 0
            self.ib = 0
            self.load_tok = {}
            self.buf_free = [[qbig_done] if i % 2 == 0 else [qbig_done2]
                             for i in range(5)]
            self.chain = {0: [qbig_done], 1: [qbig_done2]}
            self.qpt_st = [qbig_done, qbig_done2, qbig_done]
            self.t5_tok = {}
            self.last = {}

        def _load(self, i):
            par = i % 2
            sl = i % 5
            if i < self.N_GU:
                rb, hf = i // 2, i % 2
                srcap = wgu_d[rb * 128:(rb + 1) * 128,
                              hf * HT:(hf + 1) * HT]
            else:
                j = i - self.N_GU
                t, sub, hf = j // 8, (j % 8) // 2, j % 2
                srcap = wp_d[t * 128:(t + 1) * 128,
                             sub * D + hf * HT:sub * D + (hf + 1) * HT]
            self.load_tok[i] = P.dma(
                "pool", "dga%d" % par,
                (lambda e, srcap=srcap, sl=sl:
                 e.dma_start(out=qb[sl][:, :], in_=srcap)),
                deps=self.buf_free[sl])
            self.last["dga%d" % par] = self.load_tok[i]

        def _stage_a(self, i):
            par = i % 2
            sl = i % 5
            qs = i % 3
            b = qb[sl]
            ng = HT // 64
            for j in (i, i + 1, i + 2):
                if j < self.n and j not in self.load_tok:
                    self._load(j)
            t1 = P.op("dve", (lambda e, b=b, par=par: e.tensor_reduce(
                qgs[par][:, 0:ng],
                b[:, :].rearrange("p (g s) -> p g s", s=64),
                AX.X, AL.add, apply_absolute_value=True)),
                deps=[self.load_tok[i]] + self.chain[par])
            t23 = P.op("dve", (lambda e, par=par: e.tensor_scalar(
                qgc[par][:, 0:ng], qgs[par][:, 0:ng], 1.0 / 64, 1e-8,
                AL.mult, AL.max)), deps=[t1])
            t4 = P.op("dve", (lambda e, par=par: e.reciprocal(
                qgr[par][:, 0:ng], qgc[par][:, 0:ng])),
                deps=[t23, self.t5_tok.get(i - 2)])
            self.t5_tok[i] = P.op(
                "pool", (lambda e, b=b, par=par, qs=qs: e.tensor_tensor(
                    qquot[qs][:, :].rearrange("p (g s) -> p g s", s=64),
                    b[:, :].rearrange("p (g s) -> p g s", s=64),
                    qgr[par][:, 0:ng].unsqueeze(2).broadcast_to(
                        [128, ng, 64]),
                    AL.mult)), deps=[t4, self.qpt_st[qs]], sem="pl")

        def _stage_b(self, i):
            par = i % 2
            sl = i % 5
            qs = i % 3
            b = qb[sl]
            ng = HT // 64
            t6 = P.op("dve", (lambda e, b=b, qs=qs: e.tensor_scalar(
                b[:, :], qquot[qs][:, :], -0.5, None, AL.is_lt)),
                deps=[self.t5_tok[i]])
            t7a = P.op("dve", (lambda e, qs=qs: e.tensor_scalar(
                qquot[qs][:, :], qquot[qs][:, :], 0.5, None, AL.is_gt)),
                deps=[t6])
            t7 = P.op("dve", (lambda e, b=b, qs=qs: e.tensor_tensor(
                b[:, :], qquot[qs][:, :], b[:, :], AL.subtract)),
                deps=[t7a])
            t8 = P.op("dve", (lambda e, b=b, par=par: e.tensor_tensor(
                b[:, :].rearrange("p (g s) -> p g s", s=64),
                b[:, :].rearrange("p (g s) -> p g s", s=64),
                qgc[par][:, 0:ng].unsqueeze(2).broadcast_to([128, ng, 64]),
                AL.mult)), deps=[t7])
            self.chain[par] = [t8]
            tq_eng, tq_sem = ("act", "dqt%d" % par)
            tp = P.dma(tq_eng, tq_sem,
                       (lambda e, b=b, qs=qs: e.dma_start(
                           out=qpt[qs][:, :].rearrange(
                               "p (k r) -> p k r", k=DB // 2),
                           in_=b[:, :], transpose=True)),
                       deps=[t8, self.qpt_st[qs]])
            self.last[tq_sem] = tp
            self.buf_free[sl] = [tp]
            if i < self.N_GU:
                rb, hf = i // 2, i % 2
                def stg(e, rb=rb, hf=hf, qs=qs):
                    return e.dma_start(
                        out=wgqt[rb * 128:(rb + 1) * 128,
                                 hf * HT:(hf + 1) * HT],
                        in_=qpt[qs][:, :])
                ts_ = P.dma("sync", "dqs%d" % par, stg, deps=[tp])
            else:
                j = i - self.N_GU
                t, sub, hf = j // 8, (j % 8) // 2, j % 2
                def stw(e, t=t, sub=sub, hf=hf, qs=qs):
                    dst = wptdram[sub * D + hf * HT:
                                  sub * D + (hf + 1) * HT,
                                  t * 128:(t + 1) * 128]
                    dst = dst.rearrange("(k p) r -> p k r", p=128)
                    return e.dma_start(
                        out=dst, in_=qpt[qs][:, :].rearrange(
                            "p (k r) -> p k r", k=DB // 2))
                ts_ = P.dma("sync", "dqs%d" % par, stw, deps=[tp])
            self.qpt_st[qs] = ts_
            self.last["dqs%d" % par] = ts_
            if i + 2 < self.n and (i + 2) not in self.load_tok:
                self._load(i + 2)

        def step(self, k=1):
            for _ in range(k):
                if self.ia < self.n:
                    self._stage_a(self.ia)
                    self.ia += 1
                if self.ib < self.ia - 1:
                    self._stage_b(self.ib)
                    self.ib += 1

        def finish(self):
            while self.ia < self.n:
                self._stage_a(self.ia)
                self.ia += 1
            while self.ib < self.n:
                self._stage_b(self.ib)
                self.ib += 1
            return [t for t in self.last.values()] + \
                self.chain[0] + self.chain[1]

    feeder = Feeder()

    wk_loads = []
    for k in range(DB):
        tk = P.dma("sync", "dw%d" % (k % 2), (lambda e, k=k: e.dma_start(
            out=Wsb[:, k * D:(k + 1) * D],
            in_=wbf[(DB + k) * 128:(DB + k + 1) * 128, :])),
            deps=([qbig_done, qbig_done2, prep_w_done[1]] if k < 2 else ()))
        wk_loads.append(tk)
    SW = min(512, TQ)
    NST = T // SW
    k_stores = []
    kst_par = [None, None]
    kev_par = [None, None]
    for h in range(NH):
        for st in range(NST):
            par = (h * NST + st) % 2
            own = (st * SW < TQ)
            xt = XT_own if own else XT_oth
            base = st * SW - (0 if own else TQ)
            def mmk(e, h=h, par=par, xt=xt, base=base):
                ii = None
                for k in range(DB):
                    ii = e.matmul(
                        PS[:, par * 512:par * 512 + SW],
                        Wsb[:, k * D + h * 128: k * D + h * 128 + 128],
                        xt[:, k * TQ + base: k * TQ + base + SW],
                        start=(k == 0), stop=(k == DB - 1))
                return ii
            tk = P.op("pe", mmk,
                      deps=[wk_loads[-1], wk_loads[-2],
                            xt_own_done if own else xt_oth_done,
                            kev_par[par]])
            def evk(e, h=h, par=par):
                return e.activation(kvst[par][:, 0:SW],
                                    PS[:, par * 512:par * 512 + SW],
                                    AF.Identity, bias=bk_t[:, h:h + 1])
            te = P.op("act", evk, deps=[tk, t_bk, kst_par[par]])
            kev_par[par] = te
            def stk(e, h=h, st=st, par=par):
                return e.dma_start(
                    out=kdram[:, h * T + st * SW: h * T + (st + 1) * SW],
                    in_=kvst[par][:, 0:SW])
            ts_ = P.dma("sync", "dks%d" % par, stk, deps=[te])
            kst_par[par] = ts_
            k_stores.append(ts_)
            feeder.step(3)

    wv_loads = []
    for k in range(DB):
        tk = P.dma("sync", "dw%d" % (k % 2), (lambda e, k=k: e.dma_start(
            out=Wsb[:, k * D:(k + 1) * D],
            in_=wbf[(2 * DB + k) * 128:(2 * DB + k + 1) * 128, :])),
            deps=([k_stores[-1], k_stores[-2], prep_w_done[2]]
                  if k < 2 else ()))
        wv_loads.append(tk)
    v_stores = []
    vst_par = [None, None]
    vev_par = [None, None]
    for sc in range(NT):
        own = sc < NCH
        xt = XT_own if own else XT_oth
        scc = sc % NCH
        for ch in range(2):
            par = (sc * 2 + ch) % 2
            def mmv(e, xt=xt, scc=scc, ch=ch, par=par):
                ii = None
                for k in range(DB):
                    for q in range(2):
                        ii = e.matmul(
                            PS[:, (par * 2 + q) * 512:(par * 2 + q + 1) * 512],
                            xt[:, k * TQ + scc * 128: k * TQ + scc * 128 + 128],
                            Wsb[:, k * D + ch * 1024 + q * 512:
                                k * D + ch * 1024 + (q + 1) * 512],
                            start=(k == 0), stop=(k == DB - 1))
                return ii
            tv = P.op("pe", mmv,
                      deps=[wv_loads[-1], wv_loads[-2],
                            xt_own_done if own else xt_oth_done,
                            vev_par[par]])
            def evv(e, ch=ch, par=par):
                return e.tensor_tensor(kvst[par][:, 0:1024],
                                       PS[:, par * 1024: par * 1024 + 1024],
                                       bv_e[:, ch * 1024:(ch + 1) * 1024],
                                       AL.add)
            te = P.op("dve", evv, deps=[tv, vst_par[par]] + bvbo_done)
            vev_par[par] = te
            def stv(e, sc=sc, ch=ch, par=par):
                dst = vdram[ch * 8:(ch + 1) * 8, :,
                            sc * 128:(sc + 1) * 128].transpose([1, 0, 2])
                src = kvst[par][:, 0:1024].rearrange("p (h d) -> p h d", h=8)
                return e.dma_start(out=dst, in_=src)
            ts_ = P.dma("sync", "dvs%d" % par, stv, deps=[te])
            vst_par[par] = ts_
            v_stores.append(ts_)
            feeder.step(3)
    kv_dep = v_stores[-1]
    kv_dep2 = v_stores[-2]

    mload = P.dma("sync", "dc",
                  lambda e: e.dma_start(
                      out=mask_sb[:, :].rearrange("p (c q) -> p c q", c=NCH),
                      in_=mask_d.transpose([1, 0, 2])),
                  deps=[qbig_done, qbig_done2])

    kh_load, vh_load = {}, {}
    kh_free = [None, None]
    vh_free = [None, None]
    pt_free = [None, None]
    l_free = [None, None]
    av_free = None
    rs_free = None
    obig_toks = []

    def load_head(h):
        par = h % 2
        kh_load[h] = P.dma("sync", "dkh%d" % par,
                           (lambda e, h=h, par=par: e.dma_start(
                               out=Kh[par][:, :],
                               in_=kdram[:, h * T:(h + 1) * T])),
                           deps=[kv_dep, kv_dep2, kh_free[par]])
        vh_load[h] = P.dma("sync", "dvh%d" % par,
                           (lambda e, h=h, par=par: e.dma_start(
                               out=Vh[par][:, :], in_=vdram[h, :, :])),
                           deps=[kv_dep, kv_dep2, vh_free[par]])

    load_head(0)
    load_head(1)
    for h in range(NH):
        hp = h % 2
        d_last = None
        last_exp = None
        def emit_logits(gsb):
            ps = gsb // NCH
            jc = gsb % NCH
            lpar = gsb % 2
            def mml(e, h=h, gsb=gsb, jc=jc, ps=ps, lpar=lpar):
                qv = Qbig[:, :].rearrange("p (c q) -> p c q", c=DB)
                ii = None
                for rh in range(2):
                    if ps == 0:
                        e.matmul(
                            PS[:, lpar * 1024 + rh * 8 * RR:
                               lpar * 1024 + (rh + 1) * 8 * RR],
                            ident_b[:, :],
                            mask_sb[:, jc * TQ + rh * 8 * RR:
                                    jc * TQ + (rh + 1) * 8 * RR],
                            start=True, stop=False)
                    ii = e.matmul(
                        PS[:, lpar * 1024 + rh * 8 * RR:
                           lpar * 1024 + (rh + 1) * 8 * RR],
                        Kh[h % 2][:, gsb * 128:(gsb + 1) * 128],
                        qv[:, rh * 8:(rh + 1) * 8, h * RR:(h + 1) * RR],
                        start=(ps != 0), stop=True)
                return ii
            return P.op("pe", mml,
                        deps=[kh_load[h], l_free[lpar], qbig_done, t_idb,
                              mload])

        def emit_exp(gsb, tl):
            ps = gsb // NCH
            lpar = gsb % 2
            ppar = gsb % 2
            def exp_(e, ps=ps, lpar=lpar, ppar=ppar):
                b = sh1[:, 0:1] if ps == 0 else p2b[:, 0:1]
                return e.activation(Pt[ppar][:, :],
                                    PS[:, lpar * 1024: lpar * 1024 + TQ],
                                    AF.Exp, bias=b, scale=SCL)
            return P.op("act", exp_, deps=[tl, pt_free[ppar], t_p2, t_sh1])

        def emit_av(gsb, texp):
            ppar = gsb % 2
            def mmav(e, h=h, gsb=gsb, ppar=ppar):
                st = (gsb == 0)
                sp = (gsb == 2 * NCH - 1)
                ii = None
                for rh in range(NHF):
                    e.matmul(PS[0:1, 3072 + rh * 512:3072 + rh * 512 + WQ],
                             onc[:, :],
                             Pt[ppar][:, rh * 512:rh * 512 + WQ],
                             start=st, stop=sp)
                for rh in range(NHF):
                    ii = e.matmul(
                        PS[:, 2048 + rh * 512:2048 + rh * 512 + WQ],
                        Vh[h % 2][:, gsb * 128:(gsb + 1) * 128],
                        Pt[ppar][:, rh * 512:rh * 512 + WQ],
                        start=st, stop=sp)
                return ii
            return P.op("pe", mmav,
                        deps=[texp, vh_load[h], t_onc] +
                        ([av_free, rs_free] if gsb == 0 else []))

        texp_q = {}
        tl0 = emit_logits(0)
        texp_q[0] = emit_exp(0, tl0)
        l_free[0] = texp_q[0]
        for gsb in range(1, 2 * NCH):
            lpar = gsb % 2
            tl = emit_logits(gsb)
            texp_q[gsb] = emit_exp(gsb, tl)
            l_free[lpar] = texp_q[gsb]
            tav = emit_av(gsb - 1, texp_q[gsb - 1])
            pt_free[(gsb - 1) % 2] = tav
            d_last = tav
            if gsb % 2 == 0 and gsb <= 12:
                feeder.step(1)
        tav = emit_av(2 * NCH - 1, texp_q[2 * NCH - 1])
        pt_free[(2 * NCH - 1) % 2] = tav
        d_last = tav
        last_exp = texp_q[2 * NCH - 1]
        kh_free[hp] = d_last
        vh_free[hp] = d_last
        if h + 2 < NH:
            load_head(h + 2)
        tre = P.op("dve", (lambda e: e.reciprocal(
            rs_rcp[0:1, :], PS[0:1, 3072:3072 + TQ])), deps=[d_last])
        rs_free = tre
        def mmx(e):
            ii = None
            for rh in range(NHF):
                ii = e.matmul(PS[:, rh * 512:rh * 512 + WQ], onr[0:1, :],
                              rs_rcp[0:1, rh * 512:rh * 512 + WQ],
                              start=True, stop=True)
            return ii
        txp = P.op("pe", mmx,
                   deps=[tre, last_exp, l_free[0], l_free[1], t_onr])
        tcp = P.op("act", (lambda e: e.activation(
            rc_sb[:, :], PS[:, 0:TQ], AF.Copy)), deps=[txp])
        l_free[0] = tcp
        l_free[1] = tcp
        def evo(e, h=h):
            return e.tensor_tensor(Obig[:, h * TQ:(h + 1) * TQ],
                                   PS[:, 2048:2048 + TQ], rc_sb[:, :],
                                   AL.mult)
        tob = P.op("dve", evo, deps=[tcp, d_last])
        av_free = tob
        obig_toks.append(tob)
    p5_done = obig_toks[-1]
    feeder_done = feeder.finish()

    wo_loads = []
    for k in range(NH):
        tk = P.dma("sync", "dw%d" % (k % 2), (lambda e, k=k: e.dma_start(
            out=Wsb[:, k * D:(k + 1) * D],
            in_=wbf[(3 * DB + k) * 128:(3 * DB + k + 1) * 128, :])),
            deps=([p5_done, prep_w_done[3]] if k < 2 else ()))
        wo_loads.append(tk)
    acc_stores = []
    ac_par = [None, None]
    ae_par = [None, None]
    for rc in range(NCH):
        par = rc % 2
        def mmo(e, rc=rc, par=par):
            ii = None
            for dt in range(4):
                for h in range(NH):
                    ii = e.matmul(
                        PS[:, par * 2048 + dt * 512:
                           par * 2048 + (dt + 1) * 512],
                        Obig[:, h * TQ + rc * 128: h * TQ + rc * 128 + 128],
                        Wsb[:, h * D + dt * 512: h * D + (dt + 1) * 512],
                        start=(h == 0), stop=(h == NH - 1))
            return ii
        to = P.op("pe", mmo, deps=[wo_loads[-1], wo_loads[-2], p5_done, ae_par[par]])
        def eva(e, par=par):
            return e.tensor_tensor(accst[par][:, :],
                                   PS[:, par * 2048: par * 2048 + D],
                                   bo_e[:, :], AL.add)
        te = P.op("dve", eva, deps=[to, ac_par[par]] + bvbo_done +
                  (feeder_done if rc < 2 else []))
        ae_par[par] = te
        def sta(e, rc=rc, par=par):
            return e.dma_start(out=accdram[rc * 128:(rc + 1) * 128, :],
                               in_=accst[par][:, :])
        ts_ = P.dma("sync", "das%d" % par, sta, deps=[te])
        ac_par[par] = ts_
        acc_stores.append(ts_)
    p6_done = acc_stores[-1]
    p6_done2 = acc_stores[-2] if NCH > 1 else acc_stores[-1]

    wT_free = [[p6_done]] * 4
    gate_free = [[p6_done], [p6_done]]
    h_prev = [p6_done]
    wpq_done = feeder_done
    ps_free = [p6_done]
    out_toks = []
    acct_free = [None, None]
    xtf_free = [None, None]
    ost_free = [None, None]
    inv_tp_free = None

    for sh in range(NSHALF):
        srow0 = sh * NCH_H
        scol0 = sh * SHALF
        if sh > 0:
            gate_free = [list(ps_free), list(ps_free)]
            h_prev = list(ps_free)
            wT_free = [[wpT4_free[0]], [wpT4_free[1]],
                       [wpT4_free[2]], [wpT4_free[3]]]
        h2_prev = None
        hbig_toks = []
        hsum_last = None
        for j in range(NPAIR):
            par = j % 2
            tpg = P.dma("pool", "dg%d" % (par * 2),
                        (lambda e, j=j, par=par: e.dma_start(
                            out=wT[par * 2][:, :],
                            in_=wgqt[j * 128:(j + 1) * 128, :])),
                        deps=wT_free[par * 2] +
                        (feeder_done if sh == 0 and j < 2 else []))
            tpu = P.dma("pool", "dg%d" % (par * 2 + 1),
                        (lambda e, j=j, par=par: e.dma_start(
                            out=wT[par * 2 + 1][:, :],
                            in_=wgqt[FFH + j * 128:
                                     FFH + (j + 1) * 128, :])),
                        deps=wT_free[par * 2 + 1] +
                        (feeder_done if sh == 0 and j < 2 else []))
            gb = 0 if par == 0 else 4
            def mmgu(e, par=par, gb=gb, scol0=scol0):
                ii = None
                for k in range(DB):
                    for rh in range(max(1, SHALF // 512)):
                        w = min(512, SHALF)
                        e.matmul(PS[:, (gb + rh) * 512:(gb + rh) * 512 + w],
                                 wT[par * 2][:, k * 128:(k + 1) * 128],
                                 XT_own[:, k * TQ + scol0 + rh * 512:
                                        k * TQ + scol0 + rh * 512 + w],
                                 start=(k == 0), stop=(k == DB - 1))
                        ii = e.matmul(
                            PS[:, (2 + rh) * 512:(2 + rh) * 512 + w],
                            wT[par * 2 + 1][:, k * 128:(k + 1) * 128],
                            XT_own[:, k * TQ + scol0 + rh * 512:
                                   k * TQ + scol0 + rh * 512 + w],
                            start=(k == 0), stop=(k == DB - 1))
                return ii
            tmm = P.op("pe", mmgu,
                       deps=[tpg, tpu, xt_own_done] + gate_free[par] + h_prev)
            wT_free[par * 2] = [tmm]
            wT_free[par * 2 + 1] = [tmm]
            tsi = P.op("act", (lambda e, gb=gb: e.activation(
                silu_sb[:, :], PS[:, gb * 512: gb * 512 + SHALF], AF.Silu)),
                deps=[tmm])
            gate_free[par] = [tsi]
            th = P.op("dve", (lambda e, j=j: e.tensor_tensor(
                Hbig[:, j * SHALF:(j + 1) * SHALF], silu_sb[:, :],
                PS[:, 1024:1024 + SHALF], AL.mult)), deps=[tsi])
            h_prev = [th]
            hbig_toks.append(th)
            th2 = P.op("dve", (lambda e, j=j: e.tensor_tensor(
                h2_sb[:, :], Hbig[:, j * SHALF:(j + 1) * SHALF],
                Hbig[:, j * SHALF:(j + 1) * SHALF], AL.mult)),
                deps=[th, h2_prev])
            def mmh2(e, j=j):
                ii = None
                for rh in range(max(1, SHALF // 512)):
                    w = min(512, SHALF)
                    ii = e.matmul(
                        PS[0:1, 3072 + rh * 512: 3072 + rh * 512 + w],
                        onc[:, :], h2_sb[:, rh * 512:rh * 512 + w],
                        start=(j == 0), stop=(j == NPAIR - 1))
                return ii
            th3 = P.op("pe", mmh2, deps=[th2, t_onc])
            h2_prev = th3
            hsum_last = th3

        tr1 = P.op("dve", (lambda e: e.tensor_scalar(
            r1row[0:1, :], PS[0:1, 3072:3072 + SHALF], 1.0 / FFH, EPS_RMS,
            AL.mult, AL.add)), deps=[hsum_last])
        tr2 = P.op("act", (lambda e: e.activation(
            r1row[0:1, :], r1row[0:1, :], AF.Sqrt)), deps=[tr1])
        tr3 = P.op("dve", (lambda e: e.reciprocal(
            r1row[0:1, :], r1row[0:1, :])), deps=[tr2])
        tms = P.op("dve", (lambda e: e.memset(trin[:, :], 0.0)),
                   deps=[inv_tp_free])
        inv_prev = None
        psT_free = hsum_last
        for c in range(NCH_H):
            tcc = P.op("dve", (lambda e, c=c: e.tensor_copy(
                trin[0:1, :], r1row[0:1, c * 128:(c + 1) * 128])),
                deps=[tr3, tms, inv_prev])
            tt = P.op("pe", (lambda e: e.transpose(
                PS[:, 0:128], trin[:, :], ident[:, :])),
                deps=[tcc, t_id, psT_free])
            tc2 = P.op("dve", (lambda e, c=c, srow0=srow0: e.tensor_copy(
                invcol[:, srow0 + c:srow0 + c + 1], PS[:, 0:1])), deps=[tt])
            psT_free = tc2
            inv_prev = tt
        inv_done = psT_free
        inv_tp_free = inv_done

        ps_free = [inv_done] + gate_free[0] + gate_free[1] + h_prev
        wpT4_free = [wT_free[i][-1] for i in range(4)]
        for sg in range(NSG):
            last_mm = None
            for k in range(NPAIR):
                wpar = k % 4
                tw = P.dma("act" if k % 2 == 0 else "pool", "dt%d" % wpar,
                           (lambda e, k=k, wpar=wpar: e.dma_start(
                               out=wpT4[wpar][:, :],
                               in_=wptdram[k * 128:(k + 1) * 128, :])),
                           deps=[wpT4_free[wpar]] +
                           (wpq_done if sh == 0 and sg == 0 and k < 4
                            else []))
                def mmp(e, sg=sg, k=k, wpar=wpar):
                    ii = None
                    for sc in range(SCP):
                        row = sg * SCP + sc
                        for dt in range(4):
                            ii = e.matmul(
                                PS[:, sc * 2048 + dt * 512:
                                   sc * 2048 + (dt + 1) * 512],
                                Hbig[:, k * SHALF + row * 128:
                                     k * SHALF + row * 128 + 128],
                                wpT4[wpar][:, dt * 512:(dt + 1) * 512],
                                start=(k == 0), stop=(k == NPAIR - 1))
                    return ii
                tm = P.op("pe", mmp,
                          deps=[tw, hbig_toks[k]] +
                          (ps_free if k == 0 else []))
                wpT4_free[wpar] = tm
                last_mm = tm
            nps = []
            for sc in range(SCP):
                row = sg * SCP + sc
                grow = srow0 + row
                par = row % 2
                ta = P.dma("sync", "dal%d" % par,
                           (lambda e, grow=grow, par=par: e.dma_start(
                               out=acct[par][:, :],
                               in_=accdram[grow * 128:(grow + 1) * 128, :])),
                           deps=[p6_done, p6_done2, acct_free[par]])
                tx = P.dma("pool", "dxl%d" % par,
                           (lambda e, grow=grow, par=par: e.dma_start(
                               out=xt_[par][:, :],
                               in_=x_kv[grow * 128:(grow + 1) * 128, :])),
                           deps=[xtf_free[par]] +
                           ([p6_done, p6_done2] if sh == 0 else []))
                def ev1(e, sc=sc, grow=grow, par=par):
                    return e.scalar_tensor_tensor(
                        ost[par][:, :], PS[:, sc * 2048: sc * 2048 + D],
                        invcol[:, grow:grow + 1], acct[par][:, :],
                        AL.mult, AL.add)
                t1 = P.op("dve", ev1,
                          deps=[last_mm, ta, inv_done, ost_free[par]])
                acct_free[par] = t1
                def ev2(e, par=par):
                    return e.tensor_tensor(ost[par][:, :], ost[par][:, :],
                                           xt_[par][:, :], AL.add)
                t2 = P.op("dve", ev2, deps=[t1, tx])
                xtf_free[par] = t2
                def sto(e, grow=grow, par=par):
                    return e.dma_start(
                        out=out_d[grow * 128:(grow + 1) * 128, :],
                        in_=ost[par][:, :])
                ts_ = P.dma("sync", "do%d" % par, sto, deps=[t2])
                ost_free[par] = ts_
                out_toks.append(ts_)
                nps.append(t1)
            ps_free = nps

    with nc.Block() as block:
        sems = {}
        guards = []
        for s in P.sem_names:
            g = nc.semaphore(s)
            sems[s] = g.__enter__()
            guards.append(g)

        fin0 = P.counts.get("do0", 0)
        fin1 = P.counts.get("do1", 0)

        @block.tensor
        def _(t):
            P.run("pe", t, sems)

        @block.scalar
        def _(s):
            P.run("act", s, sems)

        @block.vector
        def _(v):
            P.run("dve", v, sems)

        @block.gpsimd
        def _(g):
            P.run("pool", g, sems)

        @block.sync
        def _(sy):
            P.run("sync", sy, sems)
            if fin0:
                sy.wait_ge(sems["do0"], fin0)
            if fin1:
                sy.wait_ge(sems["do1"], fin1)

    return nc



def host_prep(inputs, T):
    TQ = T // 2
    RR = T // 32
    NCH = TQ // 128
    x = np.ascontiguousarray(inputs["L0_attn_x"][:, :T, :])
    cc_i = np.arange(16)
    rr_i = np.arange(RR)
    sig = np.arange(TQ)
    s_loc = 16 * (sig % RR) + sig // RR
    mask1 = np.where(s_loc[:, None] <= s_loc[None, :], 0.0, NEG)
    mask1 = mask1.reshape(NCH, 128, TQ).astype(BF)
    ident = np.eye(128, dtype=np.float32)
    ones_col = np.ones((128, 1), dtype=BF)
    ones_row = np.ones((1, 128), dtype=np.float32)
    bq_t = np.ascontiguousarray(
        inputs["L0_attn_bq"].reshape(16, 128).T.astype(np.float32))
    bk_t = np.ascontiguousarray(
        inputs["L0_attn_bk"].reshape(16, 128).T.astype(np.float32))
    bv_r = np.ascontiguousarray(
        inputs["L0_attn_bv"].reshape(1, D).astype(np.float32))
    bo_r = np.ascontiguousarray(
        inputs["L0_attn_bo"].reshape(1, D).astype(np.float32))
    own_tok = {}
    for half in (0, 1):
        tok = (T // 2) * half + 16 * rr_i[None, :] + cc_i[:, None]
        own_tok[half] = tok.reshape(-1)
    maps = []
    for core in range(8):
        b, half = core // 2, core % 2
        own, oth = own_tok[half], own_tok[1 - half]
        x_kv = np.ascontiguousarray(x[b][np.concatenate([own, oth])])
        qrows = np.array([h * (T // 16) + RR * half + rr
                          for h in range(16) for rr in range(RR)])
        x_q = np.ascontiguousarray(x[b][qrows])
        p2b = np.full((128, 1), SHIFT + (NEG if half == 0 else 0.0),
                      dtype=np.float32)
        maps.append({
            "x_kv": x_kv, "x_q": x_q,
            "Wq": inputs["L0_attn_Wq"], "Wk": inputs["L0_attn_Wk"],
            "Wv": inputs["L0_attn_Wv"], "Wo": inputs["L0_attn_Wo"],
            "bq_t": bq_t, "bk_t": bk_t, "bv_r": bv_r, "bo_r": bo_r,
            "wgu": inputs["L0_ffn_w_gate_up"],
            "wp": inputs["L0_ffn_w_proj"],
            "mask1": mask1, "pass2b": p2b, "ident": ident,
            "ident_b": ident.astype(BF),
            "ones_col": ones_col, "ones_row": ones_row,
        })
    return maps, own_tok


def host_gather(results, own_tok, T, B=4):
    out = np.empty((B, T, D), dtype=np.float32)
    for core in range(8):
        b, half = core // 2, core % 2
        out[b][own_tok[half]] = results[core]["out"]
    return out


def kernel(**inputs):
    T = 2048
    nc = build_nc(T)
    in_maps, own_tok = host_prep(inputs, T)
    res = run_bass_kernel_spmd(nc, in_maps, core_ids=list(range(8)))
    return host_gather(res.results, own_tok, T)


# version 16 (final)
# speedup vs baseline: 2.0351x; 1.0023x over previous
import numpy as np
import ml_dtypes

import concourse.bass as bass
import concourse.mybir as mybir
from concourse.bass_utils import run_bass_kernel_spmd

BF = ml_dtypes.bfloat16
F32 = mybir.dt.float32
BF16 = mybir.dt.bfloat16
AF = mybir.ActivationFunctionType
AL = mybir.AluOpType
AX = mybir.AxisListType

D = 2048
NH = 16
DB = D // 128
FFH = 8192
NPAIR = FFH // 128
EPS_LN = 1e-5
EPS_RMS = float(np.finfo(np.float32).eps)
SCL = float(1.0 / np.float32(np.sqrt(np.float64(128))))
SHIFT = -12.0
NEG = -1.0e30


class Prog:

    def __init__(self):
        self.counts = {}
        self.ops = {k: [] for k in ("pe", "act", "dve", "pool", "sync")}

    def op(self, eng, emit, deps=(), sem=None, inc=1):
        if sem is None:
            sem = {"pe": "pe", "act": "act", "dve": "dve"}[eng]
        prev = self.counts.get(sem, 0)
        self.counts[sem] = prev + inc
        tok = (sem, self.counts[sem])
        self.ops[eng].append((emit, tuple(d for d in deps if d), sem, inc))
        return tok

    def dma(self, eng, sem, emit, deps=()):
        prev = self.counts.get(sem, 0)
        deps = list(deps) + ([(sem, prev)] if prev else [])
        return self.op(eng, emit, deps, sem=sem, inc=16)

    @property
    def sem_names(self):
        return list(self.counts.keys())

    def run(self, engname, eng, sems):
        waited = {}
        for emit, deps, sem, inc in self.ops[engname]:
            for (s, c) in deps:
                if waited.get(s, 0) < c:
                    eng.wait_ge(sems[s], c)
                    waited[s] = c
            emit(eng).then_inc(sems[sem], inc)


def build_nc(T, dbg=False):
    assert T % 256 == 0
    TQ = T // 2
    RR = T // 32
    NCH = TQ // 128
    NT = T // 128
    NHF = max(1, TQ // 512)
    WQ = min(512, TQ)
    NSHALF = 2 if NCH >= 2 else 1
    SHALF = TQ // NSHALF
    NCH_H = SHALF // 128
    SCP = 2 if NCH_H >= 2 else 1
    NSG = NCH_H // SCP

    nc = bass.Bass("TRN2", target_bir_lowering=False, debug=False)

    x_kv = nc.dram_tensor("x_kv", [T, D], F32, kind="ExternalInput").ap()
    x_q = nc.dram_tensor("x_q", [TQ, D], F32, kind="ExternalInput").ap()
    Wq_d = nc.dram_tensor("Wq", [D, D], F32, kind="ExternalInput").ap()
    Wk_d = nc.dram_tensor("Wk", [D, D], F32, kind="ExternalInput").ap()
    Wv_d = nc.dram_tensor("Wv", [D, D], F32, kind="ExternalInput").ap()
    Wo_d = nc.dram_tensor("Wo", [D, D], F32, kind="ExternalInput").ap()
    bq_d = nc.dram_tensor("bq_t", [128, DB], F32, kind="ExternalInput").ap()
    bk_d = nc.dram_tensor("bk_t", [128, NH], F32, kind="ExternalInput").ap()
    bv_d = nc.dram_tensor("bv_r", [1, D], F32, kind="ExternalInput").ap()
    bo_d = nc.dram_tensor("bo_r", [1, D], F32, kind="ExternalInput").ap()
    wgu_d = nc.dram_tensor("wgu", [2 * FFH, D], F32, kind="ExternalInput").ap()
    wp_d = nc.dram_tensor("wp", [D, FFH], F32, kind="ExternalInput").ap()
    mask_d = nc.dram_tensor("mask1", [NCH, 128, TQ], BF16,
                            kind="ExternalInput").ap()
    p2b_d = nc.dram_tensor("pass2b", [128, 1], F32, kind="ExternalInput").ap()
    id_d = nc.dram_tensor("ident", [128, 128], F32, kind="ExternalInput").ap()
    onc_d = nc.dram_tensor("ones_col", [128, 1], BF16,
                           kind="ExternalInput").ap()
    idb_d = nc.dram_tensor("ident_b", [128, 128], BF16,
                           kind="ExternalInput").ap()
    onr_d = nc.dram_tensor("ones_row", [1, 128], F32,
                           kind="ExternalInput").ap()
    out_d = nc.dram_tensor("out", [TQ, D], F32, kind="ExternalOutput").ap()
    kd_ = "ExternalOutput" if dbg else "Internal"
    wbf = nc.dram_tensor("wbf", [4 * D, D], BF16, kind=kd_).ap()
    kdram = nc.dram_tensor("kdram", [128, NH * T], BF16, kind=kd_).ap()
    vdram = nc.dram_tensor("vdram", [NH, 128, T], BF16, kind=kd_).ap()
    accdram = nc.dram_tensor("accdram", [TQ, D], F32, kind=kd_).ap()
    wptdram = nc.dram_tensor("wptdram", [FFH, D], BF16, kind=kd_).ap()
    wgqt = nc.dram_tensor("wgqt", [2 * FFH, D], BF16, kind=kd_).ap()

    SB = nc.SBUF_PARTITION_SIZE_BYTES
    BASE = 16384 + 512

    def nb(shape, dt):
        return int(np.prod(shape[1:])) * (2 if dt == BF16 else 4)

    cur = [BASE]

    def at(name, shape, dt, o=None):
        if o is None:
            o = cur[0]
            cur[0] = o + ((nb(shape, dt) + 31) // 32) * 32
        assert o >= BASE and o + nb(shape, dt) <= SB, (name, o, nb(shape, dt))
        return nc.alloc_sbuf_tensor_at(name, list(shape), dt, offset=o)

    ident = at("ident", [128, 128], F32)
    ident_b = at("ident_b", [128, 128], BF16)
    bq_t = at("bq_t", [128, DB], F32)
    bk_t = at("bk_t", [128, NH], F32)
    p2b = at("p2b", [128, 1], F32)
    onc = at("onc", [128, 1], BF16)
    onr = at("onr", [1, 128], F32)
    bv_e = at("bv_e", [128, D], BF16)
    bo_e = at("bo_e", [128, D], BF16)
    invcol = at("invcol", [128, max(NCH, 8)], F32)
    sh1 = at("sh1", [128, 1], F32)
    C_END = cur[0]
    XT_own = at("XT_own", [128, DB * TQ], BF16)
    A_END = cur[0]
    Qbig = at("Qbig", [128, DB * TQ], BF16)
    E_END = cur[0]
    XT_oth = at("XT_oth", [128, DB * TQ], BF16)
    Obig = at("Obig", [128, NH * TQ], BF16, o=E_END)
    B_END = cur[0]
    XT_q = at("XT_q", [128, DB * TQ], BF16)
    Dm0 = B_END
    mask_sb = at("mask_sb", [128, NCH * TQ], BF16, o=Dm0)
    MSK_END = Dm0 + NCH * TQ * 2
    D_END = cur[0]
    W0 = cur[0]
    Wsb = at("Wsb", [128, DB * D], BF16)
    W_END = cur[0]
    assert W_END <= SB, ("W", W_END, SB)
    o = W0
    Xf = [at("Xf%d" % i, [128, D], F32, o=o + i * D * 4) for i in range(2)]
    o += 2 * D * 4
    Xc = at("Xc", [128, D], F32, o=o); o += D * 4
    XN = [at("XN%d" % i, [128, D], BF16, o=o + i * D * 2) for i in range(2)]
    o += 2 * D * 2
    ttrj = at("ttrj", [128, D], BF16, o=o); o += D * 2
    p1c = at("p1c", [128, 8], F32, o=o); o += 32
    assert o <= W_END
    o = W0
    Kh = [at("Kh%d" % i, [128, T], BF16, o=o + i * T * 2) for i in range(2)]
    o += 2 * T * 2
    Vh = [at("Vh%d" % i, [128, T], BF16, o=o + i * T * 2) for i in range(2)]
    o += 2 * T * 2
    Pt = [at("Pt%d" % i, [128, TQ], BF16, o=o + i * TQ * 2) for i in range(2)]
    o += 2 * TQ * 2
    rs_rcp = at("rs_rcp", [1, TQ], F32, o=o); o += TQ * 4
    rc_sb = at("rc_sb", [128, TQ], F32, o=o); o += TQ * 4
    assert o <= W_END
    hb = NPAIR * SHALF * 2
    HB0 = A_END if A_END + hb <= Dm0 else W_END
    assert HB0 + hb <= SB, (HB0, hb)
    Hbig = at("Hbig", [128, NPAIR * SHALF], BF16, o=HB0)
    HB_END = HB0 + hb
    KA0 = MSK_END if MSK_END + 2 * D * 4 <= W0 else max(W_END, HB_END)
    assert KA0 + 2 * D * 4 <= SB, (KA0,)
    TOP0 = W_END
    kvst = [at("kvst%d" % i, [128, 1024], BF16, o=TOP0 + i * 1024 * 2)
            for i in range(2)]
    accst = [at("accst%d" % i, [128, D], F32, o=KA0 + i * D * 4)
             for i in range(2)]
    HT = D // 2
    qb = [at("qb%d" % i, [128, HT], BF16, o=KA0 + i * HT * 2)
          for i in range(5)]
    qquot = [at("qquot%d" % i, [128, HT], BF16, o=KA0 + (5 + i) * HT * 2)
             for i in range(3)]
    qpt = qquot
    assert KA0 + 8 * HT * 2 <= W0
    QG0 = TOP0 + 2 * 1024 * 2
    qgs = [at("qgs%d" % i, [128, HT // 64], F32, o=QG0 + i * 64)
           for i in range(2)]
    qgc = [at("qgc%d" % i, [128, HT // 64], BF16, o=QG0 + 128 + i * 32)
           for i in range(2)]
    qgr = [at("qgr%d" % i, [128, HT // 64], F32, o=QG0 + 192 + i * 64)
           for i in range(2)]
    assert QG0 + 320 <= SB
    wcst = [at("wcst%d" % i, [128, D], BF16, o=W0 + 14 * D * 2 + i * D * 2)
            for i in range(2)]
    o = W0
    wnat = [at("wnat%d" % i, [128, D], BF16, o=o + i * D * 2)
            for i in range(4)]
    WNAT0 = o
    o += 4 * D * 2
    quot = at("quot", [128, D], BF16, o=o); o += D * 2
    m2b = at("m2b", [128, D], BF16, o=o); o += D * 2
    gsum = at("gsum", [128, D // 64], F32, o=o); o += (D // 64) * 4
    gsc = at("gsc", [128, D // 64], BF16, o=o); o += (D // 64) * 4
    grc = at("grc", [128, D // 64], F32, o=o); o += (D // 64) * 4
    wT = [at("wT%d" % i, [128, D], BF16, o=o + i * D * 2) for i in range(4)]
    WT0 = o
    o += 4 * D * 2
    wpt = [at("wpt%d" % i, [128, D], BF16, o=o + i * D * 2) for i in range(2)]
    o += 2 * D * 2
    silu_sb = at("silu_sb", [128, SHALF], BF16, o=o); o += SHALF * 2
    h2_sb = at("h2_sb", [128, SHALF], BF16, o=o); o += SHALF * 2
    r1row = at("r1row", [1, SHALF], F32, o=o); o += SHALF * 4
    trin = at("trin", [128, 128], F32, o=o); o += 128 * 4
    assert o <= W_END, ("p7", o, W_END)
    B9 = Dm0 if Dm0 + 4 * D * 4 <= W0 else KA0 + 2 * D * 4
    assert B9 + 4 * D * 4 <= SB, (B9,)
    assert B9 >= W0 or B9 + 4 * D * 4 <= W0 or Dm0 + 4 * D * 4 <= W0
    acct = [at("acct%d" % i, [128, D], F32, o=B9 + i * D * 4)
            for i in range(2)]
    xt_ = [at("xt_%d" % i, [128, D], F32, o=B9 + (2 + i) * D * 4)
           for i in range(2)]
    ost = [at("ost%d" % i, [128, D], F32, o=WNAT0 + i * D * 4)
           for i in range(2)]
    wpT4 = [at("wpT%d" % i, [128, D], BF16, o=WT0 + i * D * 2)
            for i in range(4)]

    PS = nc.alloc_psum_tensor("PS", [128, 4096], F32)
    P = Prog()

    t_id = P.dma("sync", "dc", lambda e: e.dma_start(out=ident[:, :], in_=id_d))
    t_idb = P.dma("sync", "dc", lambda e: e.dma_start(out=ident_b[:, :], in_=idb_d))
    t_bq = P.dma("sync", "dc", lambda e: e.dma_start(out=bq_t[:, :], in_=bq_d))
    t_bk = P.dma("sync", "dc", lambda e: e.dma_start(out=bk_t[:, :], in_=bk_d))
    t_p2 = P.dma("sync", "dc", lambda e: e.dma_start(out=p2b[:, :], in_=p2b_d))
    t_onc = P.dma("sync", "dc", lambda e: e.dma_start(out=onc[:, :], in_=onc_d))
    t_onr = P.dma("sync", "dc", lambda e: e.dma_start(out=onr[:, :], in_=onr_d))
    t_bv = P.dma("sync", "dc", lambda e: e.dma_start(out=Xf[0][0:1, :], in_=bv_d))
    t_bo = P.dma("sync", "dc", lambda e: e.dma_start(out=Xf[1][0:1, :], in_=bo_d))
    t_sh1 = P.op("dve", lambda e: e.memset(sh1[:, :], SHIFT))

    bank_dep = [None] * 4
    for src_t, stg, buf in ((t_bv, Xf[0], bv_e), (t_bo, Xf[1], bo_e)):
        for i in range(D // 512):
            bk_i = i % 4
            def em(e, stg=stg, i=i, bk_i=bk_i):
                return e.matmul(PS[:, bk_i * 512:(bk_i + 1) * 512],
                                onr[0:1, :], stg[0:1, i * 512:(i + 1) * 512],
                                start=True, stop=True)
            tmk = P.op("pe", em, deps=[src_t, t_onr, bank_dep[bk_i]])
            def ev(e, buf=buf, i=i, bk_i=bk_i):
                return e.tensor_copy(buf[:, i * 512:(i + 1) * 512],
                                     PS[:, bk_i * 512:(bk_i + 1) * 512])
            bank_dep[bk_i] = P.op("dve", ev, deps=[tmk])
    bvbo_done = [t for t in bank_dep if t]

    prep_srcs = [Wq_d, Wk_d, Wv_d, Wo_d]
    prep_last = [None, None]
    prep_w_done = {}
    prep_state = {"i": 0}

    def prep_step(k=1):
        for _ in range(k):
            i = prep_state["i"]
            if i >= 4 * DB:
                return
            prep_state["i"] = i + 1
            w, t = i // DB, i % DB
            par = i % 2
            tl = P.dma("pool", "dwc%d" % par,
                       (lambda e, w=w, t=t, par=par: e.dma_start(
                           out=wcst[par][:, :],
                           in_=prep_srcs[w][t * 128:(t + 1) * 128, :])),
                       deps=[prep_last[par]])
            ts_ = P.dma("sync", "dws%d" % par,
                        (lambda e, i=i, par=par: e.dma_start(
                            out=wbf[i * 128:(i + 1) * 128, :],
                            in_=wcst[par][:, :])), deps=[tl])
            prep_last[par] = ts_
            prep_w_done[w] = ts_

    n_tiles = NT + NCH
    xbar_toks = []
    ln_dve_last = None
    buf_free = [None, None]
    xn_free = [None, None]

    for i in range(n_tiles):
        par = i % 2
        src = (x_kv[i * 128:(i + 1) * 128, :] if i < NT
               else x_q[(i - NT) * 128:(i - NT + 1) * 128, :])
        tl = P.dma("sync", "dx%d" % par,
                   (lambda e, src=src, par=par:
                    e.dma_start(out=Xf[par][:, :], in_=src)),
                   deps=[buf_free[par]] + (bvbo_done if i < 2 else []))
        t1 = P.op("dve", (lambda e, par=par: e.reduce_sum(
            p1c[:, 0:1], Xf[par][:, :], AX.X)), deps=[tl, ln_dve_last])
        t2 = P.op("dve", lambda e: e.tensor_scalar(
            p1c[:, 1:2], p1c[:, 0:1], 1.0 / D, None, AL.mult), deps=[t1])
        t3 = P.op("dve", (lambda e, par=par: e.tensor_scalar(
            Xc[:, :], Xf[par][:, :], p1c[:, 1:2], None, AL.subtract)),
            deps=[t2])
        buf_free[par] = t3
        t4 = P.op("act", lambda e: e.activation(
            ttrj[:, :], Xc[:, :], AF.Square, accum_out=p1c[:, 2:3]),
            deps=[t3])
        t5 = P.op("dve", lambda e: e.tensor_scalar(
            p1c[:, 3:4], p1c[:, 2:3], 1.0 / D, EPS_LN, AL.mult, AL.add),
            deps=[t4])
        t6 = P.op("act", lambda e: e.activation(
            p1c[:, 4:5], p1c[:, 3:4], AF.Sqrt), deps=[t5])
        t7 = P.op("dve", lambda e: e.reciprocal(
            p1c[:, 5:6], p1c[:, 4:5]), deps=[t6])
        t8 = P.op("dve", (lambda e, par=par: e.tensor_scalar(
            XN[par][:, :], Xc[:, :], p1c[:, 5:6], None, AL.mult)),
            deps=[t7, xn_free[par]])
        ln_dve_last = t8
        if i < NCH:
            dsti, col = XT_own, i * 128
        elif i < NT:
            dsti, col = XT_oth, (i - NCH) * 128
        else:
            dsti, col = XT_q, (i - NT) * 128
        def tp(e, par=par, dsti=dsti, col=col):
            dst = dsti[:, :].rearrange("p (k s) -> p k s", k=DB)[:, :,
                                                                col:col + 128]
            return e.dma_start(out=dst, in_=XN[par][:, :], transpose=True)
        tk = P.dma("act", "dt%d" % par, tp, deps=[t8])
        xn_free[par] = tk
        xbar_toks.append(tk)
        prep_step(3)

    while prep_state["i"] < 4 * DB:
        prep_step(1)

    xt_own_done = xbar_toks[NCH - 1]
    xt_oth_done = xbar_toks[NT - 1]
    xt_q_done = xbar_toks[NT + NCH - 1]
    p1_done = [xbar_toks[-1], xbar_toks[-2], ln_dve_last]

    wq_loads = []
    for k in range(DB):
        tk = P.dma("sync", "dw%d" % (k % 2), (lambda e, k=k: e.dma_start(
            out=Wsb[:, k * D:(k + 1) * D],
            in_=wbf[k * 128:(k + 1) * 128, :])),
            deps=((p1_done + [prep_w_done[0]]) if k < 2 else
                  (prep_last if k >= DB - 2 else ())))
        wq_loads.append(tk)
    q_evac = []
    q_ev_par = [bvbo_done[-1], bvbo_done[-1]]
    for cc in range(DB):
        par = cc % 2
        def mmq(e, cc=cc, par=par):
            ii = None
            for k in range(DB):
                for hf in range(NHF):
                    ii = e.matmul(
                        PS[:, (par * 2 + hf) * 512:(par * 2 + hf) * 512 + WQ],
                        Wsb[:, k * D + cc * 128: k * D + cc * 128 + 128],
                        XT_q[:, k * TQ + hf * 512: k * TQ + hf * 512 + WQ],
                        start=(k == 0), stop=(k == DB - 1))
            return ii
        tq = P.op("pe", mmq, deps=[wq_loads[-1], wq_loads[-2], xt_q_done, q_ev_par[par]])
        def evq(e, cc=cc, par=par):
            return e.activation(Qbig[:, cc * TQ:(cc + 1) * TQ],
                                PS[:, par * 1024: par * 1024 + TQ],
                                AF.Identity, bias=bq_t[:, cc:cc + 1])
        te = P.op("act", evq, deps=[tq, t_bq])
        q_ev_par[par] = te
        q_evac.append(te)
    qbig_done = q_evac[-1]
    qbig_done2 = q_evac[-2]

    class Feeder:

        N_GU = 2 * (2 * FFH // 128)
        N_WP = 2 * DB * (FFH // D)

        def __init__(self):
            self.n = self.N_GU + self.N_WP
            self.ia = 0
            self.ib = 0
            self.load_tok = {}
            self.buf_free = [[qbig_done] if i % 2 == 0 else [qbig_done2]
                             for i in range(5)]
            self.chain = {0: [qbig_done], 1: [qbig_done2]}
            self.qpt_st = [qbig_done, qbig_done2, qbig_done]
            self.t5_tok = {}
            self.last = {}

        def _load(self, i):
            par = i % 2
            sl = i % 5
            if i < self.N_GU:
                rb, hf = i // 2, i % 2
                srcap = wgu_d[rb * 128:(rb + 1) * 128,
                              hf * HT:(hf + 1) * HT]
            else:
                j = i - self.N_GU
                t, sub, hf = j // 8, (j % 8) // 2, j % 2
                srcap = wp_d[t * 128:(t + 1) * 128,
                             sub * D + hf * HT:sub * D + (hf + 1) * HT]
            self.load_tok[i] = P.dma(
                "pool", "dga%d" % par,
                (lambda e, srcap=srcap, sl=sl:
                 e.dma_start(out=qb[sl][:, :], in_=srcap)),
                deps=self.buf_free[sl])
            self.last["dga%d" % par] = self.load_tok[i]

        def _stage_a(self, i):
            par = i % 2
            sl = i % 5
            qs = i % 3
            b = qb[sl]
            ng = HT // 64
            for j in (i, i + 1, i + 2):
                if j < self.n and j not in self.load_tok:
                    self._load(j)
            t1 = P.op("dve", (lambda e, b=b, par=par: e.tensor_reduce(
                qgs[par][:, 0:ng],
                b[:, :].rearrange("p (g s) -> p g s", s=64),
                AX.X, AL.add, apply_absolute_value=True)),
                deps=[self.load_tok[i]] + self.chain[par])
            t23 = P.op("dve", (lambda e, par=par: e.tensor_scalar(
                qgc[par][:, 0:ng], qgs[par][:, 0:ng], 1.0 / 64, 1e-8,
                AL.mult, AL.max)), deps=[t1])
            t4 = P.op("dve", (lambda e, par=par: e.reciprocal(
                qgr[par][:, 0:ng], qgc[par][:, 0:ng])),
                deps=[t23, self.t5_tok.get(i - 2)])
            self.t5_tok[i] = P.op(
                "pool", (lambda e, b=b, par=par, qs=qs: e.tensor_tensor(
                    qquot[qs][:, :].rearrange("p (g s) -> p g s", s=64),
                    b[:, :].rearrange("p (g s) -> p g s", s=64),
                    qgr[par][:, 0:ng].unsqueeze(2).broadcast_to(
                        [128, ng, 64]),
                    AL.mult)), deps=[t4, self.qpt_st[qs]], sem="pl")

        def _stage_b(self, i):
            par = i % 2
            sl = i % 5
            qs = i % 3
            b = qb[sl]
            ng = HT // 64
            t6 = P.op("dve", (lambda e, b=b, qs=qs: e.tensor_scalar(
                b[:, :], qquot[qs][:, :], -0.5, None, AL.is_lt)),
                deps=[self.t5_tok[i]])
            t7a = P.op("dve", (lambda e, qs=qs: e.tensor_scalar(
                qquot[qs][:, :], qquot[qs][:, :], 0.5, None, AL.is_gt)),
                deps=[t6])
            t7 = P.op("dve", (lambda e, b=b, qs=qs: e.tensor_tensor(
                b[:, :], qquot[qs][:, :], b[:, :], AL.subtract)),
                deps=[t7a])
            t8 = P.op("dve", (lambda e, b=b, par=par: e.tensor_tensor(
                b[:, :].rearrange("p (g s) -> p g s", s=64),
                b[:, :].rearrange("p (g s) -> p g s", s=64),
                qgc[par][:, 0:ng].unsqueeze(2).broadcast_to([128, ng, 64]),
                AL.mult)), deps=[t7])
            self.chain[par] = [t8]
            tq_eng, tq_sem = ("act", "dqt%d" % par)
            tp = P.dma(tq_eng, tq_sem,
                       (lambda e, b=b, qs=qs: e.dma_start(
                           out=qpt[qs][:, :].rearrange(
                               "p (k r) -> p k r", k=DB // 2),
                           in_=b[:, :], transpose=True)),
                       deps=[t8, self.qpt_st[qs]])
            self.last[tq_sem] = tp
            self.buf_free[sl] = [tp]
            if i < self.N_GU:
                rb, hf = i // 2, i % 2
                def stg(e, rb=rb, hf=hf, qs=qs):
                    return e.dma_start(
                        out=wgqt[rb * 128:(rb + 1) * 128,
                                 hf * HT:(hf + 1) * HT],
                        in_=qpt[qs][:, :])
                ts_ = P.dma("sync", "dqs%d" % par, stg, deps=[tp])
            else:
                j = i - self.N_GU
                t, sub, hf = j // 8, (j % 8) // 2, j % 2
                def stw(e, t=t, sub=sub, hf=hf, qs=qs):
                    dst = wptdram[sub * D + hf * HT:
                                  sub * D + (hf + 1) * HT,
                                  t * 128:(t + 1) * 128]
                    dst = dst.rearrange("(k p) r -> p k r", p=128)
                    return e.dma_start(
                        out=dst, in_=qpt[qs][:, :].rearrange(
                            "p (k r) -> p k r", k=DB // 2))
                ts_ = P.dma("sync", "dqs%d" % par, stw, deps=[tp])
            self.qpt_st[qs] = ts_
            self.last["dqs%d" % par] = ts_
            if i + 2 < self.n and (i + 2) not in self.load_tok:
                self._load(i + 2)

        def step(self, k=1):
            for _ in range(k):
                if self.ia < self.n:
                    self._stage_a(self.ia)
                    self.ia += 1
                if self.ib < self.ia - 1:
                    self._stage_b(self.ib)
                    self.ib += 1

        def finish(self):
            while self.ia < self.n:
                self._stage_a(self.ia)
                self.ia += 1
            while self.ib < self.n:
                self._stage_b(self.ib)
                self.ib += 1
            return [t for t in self.last.values()] + \
                self.chain[0] + self.chain[1]

    feeder = Feeder()

    wk_loads = []
    for k in range(DB):
        tk = P.dma("sync", "dw%d" % (k % 2), (lambda e, k=k: e.dma_start(
            out=Wsb[:, k * D:(k + 1) * D],
            in_=wbf[(DB + k) * 128:(DB + k + 1) * 128, :])),
            deps=([qbig_done, qbig_done2, prep_w_done[1]] if k < 2 else ()))
        wk_loads.append(tk)
    SW = min(512, TQ)
    NST = T // SW
    k_stores = []
    kst_par = [None, None]
    kev_par = [None, None]
    for h in range(NH):
        for st in range(NST):
            par = (h * NST + st) % 2
            own = (st * SW < TQ)
            xt = XT_own if own else XT_oth
            base = st * SW - (0 if own else TQ)
            def mmk(e, h=h, par=par, xt=xt, base=base):
                ii = None
                for k in range(DB):
                    ii = e.matmul(
                        PS[:, par * 512:par * 512 + SW],
                        Wsb[:, k * D + h * 128: k * D + h * 128 + 128],
                        xt[:, k * TQ + base: k * TQ + base + SW],
                        start=(k == 0), stop=(k == DB - 1))
                return ii
            tk = P.op("pe", mmk,
                      deps=[wk_loads[-1], wk_loads[-2],
                            xt_own_done if own else xt_oth_done,
                            kev_par[par]])
            def evk(e, h=h, par=par):
                return e.activation(kvst[par][:, 0:SW],
                                    PS[:, par * 512:par * 512 + SW],
                                    AF.Identity, bias=bk_t[:, h:h + 1])
            te = P.op("act", evk, deps=[tk, t_bk, kst_par[par]])
            kev_par[par] = te
            def stk(e, h=h, st=st, par=par):
                return e.dma_start(
                    out=kdram[:, h * T + st * SW: h * T + (st + 1) * SW],
                    in_=kvst[par][:, 0:SW])
            ts_ = P.dma("sync", "dks%d" % par, stk, deps=[te])
            kst_par[par] = ts_
            k_stores.append(ts_)
            feeder.step(3)

    wv_loads = []
    for k in range(DB):
        tk = P.dma("sync", "dw%d" % (k % 2), (lambda e, k=k: e.dma_start(
            out=Wsb[:, k * D:(k + 1) * D],
            in_=wbf[(2 * DB + k) * 128:(2 * DB + k + 1) * 128, :])),
            deps=([k_stores[-1], k_stores[-2], prep_w_done[2]]
                  if k < 2 else ()))
        wv_loads.append(tk)
    v_stores = []
    vst_par = [None, None]
    vev_par = [None, None]
    for sc in range(NT):
        own = sc < NCH
        xt = XT_own if own else XT_oth
        scc = sc % NCH
        for ch in range(2):
            par = (sc * 2 + ch) % 2
            def mmv(e, xt=xt, scc=scc, ch=ch, par=par):
                ii = None
                for k in range(DB):
                    for q in range(2):
                        ii = e.matmul(
                            PS[:, (par * 2 + q) * 512:(par * 2 + q + 1) * 512],
                            xt[:, k * TQ + scc * 128: k * TQ + scc * 128 + 128],
                            Wsb[:, k * D + ch * 1024 + q * 512:
                                k * D + ch * 1024 + (q + 1) * 512],
                            start=(k == 0), stop=(k == DB - 1))
                return ii
            tv = P.op("pe", mmv,
                      deps=[wv_loads[-1], wv_loads[-2],
                            xt_own_done if own else xt_oth_done,
                            vev_par[par]])
            def evv(e, ch=ch, par=par):
                return e.tensor_tensor(kvst[par][:, 0:1024],
                                       PS[:, par * 1024: par * 1024 + 1024],
                                       bv_e[:, ch * 1024:(ch + 1) * 1024],
                                       AL.add)
            te = P.op("dve", evv, deps=[tv, vst_par[par]] + bvbo_done)
            vev_par[par] = te
            def stv(e, sc=sc, ch=ch, par=par):
                dst = vdram[ch * 8:(ch + 1) * 8, :,
                            sc * 128:(sc + 1) * 128].transpose([1, 0, 2])
                src = kvst[par][:, 0:1024].rearrange("p (h d) -> p h d", h=8)
                return e.dma_start(out=dst, in_=src)
            ts_ = P.dma("sync", "dvs%d" % par, stv, deps=[te])
            vst_par[par] = ts_
            v_stores.append(ts_)
            feeder.step(3)
    kv_dep = v_stores[-1]
    kv_dep2 = v_stores[-2]

    mload = P.dma("sync", "dc",
                  lambda e: e.dma_start(
                      out=mask_sb[:, :].rearrange("p (c q) -> p c q", c=NCH),
                      in_=mask_d.transpose([1, 0, 2])),
                  deps=[qbig_done, qbig_done2])

    kh_load, vh_load = {}, {}
    kh_free = [None, None]
    vh_free = [None, None]
    pt_free = [None, None]
    l_free = [None, None]
    av_free = None
    rs_free = None
    obig_toks = []

    def load_head(h):
        par = h % 2
        kh_load[h] = P.dma("sync", "dkh%d" % par,
                           (lambda e, h=h, par=par: e.dma_start(
                               out=Kh[par][:, :],
                               in_=kdram[:, h * T:(h + 1) * T])),
                           deps=[kv_dep, kv_dep2, kh_free[par]])
        vh_load[h] = P.dma("sync", "dvh%d" % par,
                           (lambda e, h=h, par=par: e.dma_start(
                               out=Vh[par][:, :], in_=vdram[h, :, :])),
                           deps=[kv_dep, kv_dep2, vh_free[par]])

    load_head(0)
    load_head(1)
    for h in range(NH):
        hp = h % 2
        d_last = None
        last_exp = None
        def emit_logits(gsb):
            ps = gsb // NCH
            jc = gsb % NCH
            lpar = gsb % 2
            def mml(e, h=h, gsb=gsb, jc=jc, ps=ps, lpar=lpar):
                qv = Qbig[:, :].rearrange("p (c q) -> p c q", c=DB)
                ii = None
                for rh in range(2):
                    if ps == 0:
                        e.matmul(
                            PS[:, lpar * 1024 + rh * 8 * RR:
                               lpar * 1024 + (rh + 1) * 8 * RR],
                            ident_b[:, :],
                            mask_sb[:, jc * TQ + rh * 8 * RR:
                                    jc * TQ + (rh + 1) * 8 * RR],
                            start=True, stop=False)
                    ii = e.matmul(
                        PS[:, lpar * 1024 + rh * 8 * RR:
                           lpar * 1024 + (rh + 1) * 8 * RR],
                        Kh[h % 2][:, gsb * 128:(gsb + 1) * 128],
                        qv[:, rh * 8:(rh + 1) * 8, h * RR:(h + 1) * RR],
                        start=(ps != 0), stop=True)
                return ii
            return P.op("pe", mml,
                        deps=[kh_load[h], l_free[lpar], qbig_done, t_idb,
                              mload])

        def emit_exp(gsb, tl):
            ps = gsb // NCH
            lpar = gsb % 2
            ppar = gsb % 2
            def exp_(e, ps=ps, lpar=lpar, ppar=ppar):
                b = sh1[:, 0:1] if ps == 0 else p2b[:, 0:1]
                return e.activation(Pt[ppar][:, :],
                                    PS[:, lpar * 1024: lpar * 1024 + TQ],
                                    AF.Exp, bias=b, scale=SCL)
            return P.op("act", exp_, deps=[tl, pt_free[ppar], t_p2, t_sh1])

        def emit_av(gsb, texp):
            ppar = gsb % 2
            def mmav(e, h=h, gsb=gsb, ppar=ppar):
                st = (gsb == 0)
                sp = (gsb == 2 * NCH - 1)
                ii = None
                for rh in range(NHF):
                    e.matmul(PS[0:1, 3072 + rh * 512:3072 + rh * 512 + WQ],
                             onc[:, :],
                             Pt[ppar][:, rh * 512:rh * 512 + WQ],
                             start=st, stop=sp)
                for rh in range(NHF):
                    ii = e.matmul(
                        PS[:, 2048 + rh * 512:2048 + rh * 512 + WQ],
                        Vh[h % 2][:, gsb * 128:(gsb + 1) * 128],
                        Pt[ppar][:, rh * 512:rh * 512 + WQ],
                        start=st, stop=sp)
                return ii
            return P.op("pe", mmav,
                        deps=[texp, vh_load[h], t_onc] +
                        ([av_free, rs_free] if gsb == 0 else []))

        texp_q = {}
        tl0 = emit_logits(0)
        texp_q[0] = emit_exp(0, tl0)
        l_free[0] = texp_q[0]
        for gsb in range(1, 2 * NCH):
            lpar = gsb % 2
            tl = emit_logits(gsb)
            texp_q[gsb] = emit_exp(gsb, tl)
            l_free[lpar] = texp_q[gsb]
            tav = emit_av(gsb - 1, texp_q[gsb - 1])
            pt_free[(gsb - 1) % 2] = tav
            d_last = tav
            if gsb % 2 == 0 and gsb <= 12:
                feeder.step(1)
        tav = emit_av(2 * NCH - 1, texp_q[2 * NCH - 1])
        pt_free[(2 * NCH - 1) % 2] = tav
        d_last = tav
        last_exp = texp_q[2 * NCH - 1]
        kh_free[hp] = d_last
        vh_free[hp] = d_last
        if h + 2 < NH:
            load_head(h + 2)
        tre = P.op("dve", (lambda e: e.reciprocal(
            rs_rcp[0:1, :], PS[0:1, 3072:3072 + TQ])), deps=[d_last])
        rs_free = tre
        def mmx(e):
            ii = None
            for rh in range(NHF):
                ii = e.matmul(PS[:, rh * 512:rh * 512 + WQ], onr[0:1, :],
                              rs_rcp[0:1, rh * 512:rh * 512 + WQ],
                              start=True, stop=True)
            return ii
        txp = P.op("pe", mmx,
                   deps=[tre, last_exp, l_free[0], l_free[1], t_onr])
        tcp = P.op("act", (lambda e: e.activation(
            rc_sb[:, :], PS[:, 0:TQ], AF.Copy)), deps=[txp])
        l_free[0] = tcp
        l_free[1] = tcp
        def evo(e, h=h):
            return e.tensor_tensor(Obig[:, h * TQ:(h + 1) * TQ],
                                   PS[:, 2048:2048 + TQ], rc_sb[:, :],
                                   AL.mult)
        tob = P.op("dve", evo, deps=[tcp, d_last])
        av_free = tob
        obig_toks.append(tob)
    p5_done = obig_toks[-1]
    feeder_done = feeder.finish()

    wo_loads = []
    for k in range(NH):
        tk = P.dma("sync", "dw%d" % (k % 2), (lambda e, k=k: e.dma_start(
            out=Wsb[:, k * D:(k + 1) * D],
            in_=wbf[(3 * DB + k) * 128:(3 * DB + k + 1) * 128, :])),
            deps=([p5_done, prep_w_done[3]] if k < 2 else ()))
        wo_loads.append(tk)
    acc_stores = []
    ac_par = [None, None]
    ae_par = [None, None]
    for rc in range(NCH):
        par = rc % 2
        def mmo(e, rc=rc, par=par):
            ii = None
            for dt in range(4):
                for h in range(NH):
                    ii = e.matmul(
                        PS[:, par * 2048 + dt * 512:
                           par * 2048 + (dt + 1) * 512],
                        Obig[:, h * TQ + rc * 128: h * TQ + rc * 128 + 128],
                        Wsb[:, h * D + dt * 512: h * D + (dt + 1) * 512],
                        start=(h == 0), stop=(h == NH - 1))
            return ii
        to = P.op("pe", mmo, deps=[wo_loads[-1], wo_loads[-2], p5_done, ae_par[par]])
        def eva(e, par=par):
            return e.tensor_tensor(accst[par][:, :],
                                   PS[:, par * 2048: par * 2048 + D],
                                   bo_e[:, :], AL.add)
        te = P.op("dve", eva, deps=[to, ac_par[par]] + bvbo_done +
                  (feeder_done if rc < 2 else []))
        ae_par[par] = te
        def sta(e, rc=rc, par=par):
            return e.dma_start(out=accdram[rc * 128:(rc + 1) * 128, :],
                               in_=accst[par][:, :])
        ts_ = P.dma("sync", "das%d" % par, sta, deps=[te])
        ac_par[par] = ts_
        acc_stores.append(ts_)
    p6_done = acc_stores[-1]
    p6_done2 = acc_stores[-2] if NCH > 1 else acc_stores[-1]

    wT_free = [[p6_done]] * 4
    gate_free = [[p6_done], [p6_done]]
    h_prev = [p6_done]
    wpq_done = feeder_done
    ps_free = [p6_done]
    out_toks = []
    acct_free = [None, None]
    xtf_free = [None, None]
    ost_free = [None, None]
    inv_tp_free = None

    for sh in range(NSHALF):
        srow0 = sh * NCH_H
        scol0 = sh * SHALF
        if sh > 0:
            gate_free = [list(ps_free), list(ps_free)]
            h_prev = list(ps_free)
            wT_free = [[wpT4_free[0]], [wpT4_free[1]],
                       [wpT4_free[2]], [wpT4_free[3]]]
        h2_prev = None
        hbig_toks = []
        hsum_last = None
        for j in range(NPAIR):
            par = j % 2
            tpg = P.dma("pool", "dg%d" % (par * 2),
                        (lambda e, j=j, par=par: e.dma_start(
                            out=wT[par * 2][:, :],
                            in_=wgqt[j * 128:(j + 1) * 128, :])),
                        deps=wT_free[par * 2] +
                        (feeder_done if sh == 0 and j < 2 else []))
            tpu = P.dma("pool", "dg%d" % (par * 2 + 1),
                        (lambda e, j=j, par=par: e.dma_start(
                            out=wT[par * 2 + 1][:, :],
                            in_=wgqt[FFH + j * 128:
                                     FFH + (j + 1) * 128, :])),
                        deps=wT_free[par * 2 + 1] +
                        (feeder_done if sh == 0 and j < 2 else []))
            gb = 0 if par == 0 else 4
            def mmgu(e, par=par, gb=gb, scol0=scol0):
                ii = None
                for k in range(DB):
                    for rh in range(max(1, SHALF // 512)):
                        w = min(512, SHALF)
                        e.matmul(PS[:, (gb + rh) * 512:(gb + rh) * 512 + w],
                                 wT[par * 2][:, k * 128:(k + 1) * 128],
                                 XT_own[:, k * TQ + scol0 + rh * 512:
                                        k * TQ + scol0 + rh * 512 + w],
                                 start=(k == 0), stop=(k == DB - 1))
                        ii = e.matmul(
                            PS[:, (2 + rh) * 512:(2 + rh) * 512 + w],
                            wT[par * 2 + 1][:, k * 128:(k + 1) * 128],
                            XT_own[:, k * TQ + scol0 + rh * 512:
                                   k * TQ + scol0 + rh * 512 + w],
                            start=(k == 0), stop=(k == DB - 1))
                return ii
            tmm = P.op("pe", mmgu,
                       deps=[tpg, tpu, xt_own_done] + gate_free[par] + h_prev)
            wT_free[par * 2] = [tmm]
            wT_free[par * 2 + 1] = [tmm]
            tsi = P.op("act", (lambda e, gb=gb: e.activation(
                silu_sb[:, :], PS[:, gb * 512: gb * 512 + SHALF], AF.Silu)),
                deps=[tmm])
            gate_free[par] = [tsi]
            th = P.op("dve", (lambda e, j=j: e.tensor_tensor(
                Hbig[:, j * SHALF:(j + 1) * SHALF], silu_sb[:, :],
                PS[:, 1024:1024 + SHALF], AL.mult)), deps=[tsi])
            h_prev = [th]
            hbig_toks.append(th)
            th2 = P.op("dve", (lambda e, j=j: e.tensor_tensor(
                h2_sb[:, :], Hbig[:, j * SHALF:(j + 1) * SHALF],
                Hbig[:, j * SHALF:(j + 1) * SHALF], AL.mult)),
                deps=[th, h2_prev])
            def mmh2(e, j=j):
                ii = None
                for rh in range(max(1, SHALF // 512)):
                    w = min(512, SHALF)
                    ii = e.matmul(
                        PS[0:1, 3072 + rh * 512: 3072 + rh * 512 + w],
                        onc[:, :], h2_sb[:, rh * 512:rh * 512 + w],
                        start=(j == 0), stop=(j == NPAIR - 1))
                return ii
            th3 = P.op("pe", mmh2, deps=[th2, t_onc])
            h2_prev = th3
            hsum_last = th3

        tr1 = P.op("dve", (lambda e: e.tensor_scalar(
            r1row[0:1, :], PS[0:1, 3072:3072 + SHALF], 1.0 / FFH, EPS_RMS,
            AL.mult, AL.add)), deps=[hsum_last])
        tr2 = P.op("act", (lambda e: e.activation(
            r1row[0:1, :], r1row[0:1, :], AF.Sqrt)), deps=[tr1])
        tr3 = P.op("dve", (lambda e: e.reciprocal(
            r1row[0:1, :], r1row[0:1, :])), deps=[tr2])
        tms = P.op("dve", (lambda e: e.memset(trin[:, :], 0.0)),
                   deps=[inv_tp_free])
        inv_prev = None
        psT_free = hsum_last
        for c in range(NCH_H):
            tcc = P.op("dve", (lambda e, c=c: e.tensor_copy(
                trin[0:1, :], r1row[0:1, c * 128:(c + 1) * 128])),
                deps=[tr3, tms, inv_prev])
            tt = P.op("pe", (lambda e: e.transpose(
                PS[:, 0:128], trin[:, :], ident[:, :])),
                deps=[tcc, t_id, psT_free])
            tc2 = P.op("dve", (lambda e, c=c, srow0=srow0: e.tensor_copy(
                invcol[:, srow0 + c:srow0 + c + 1], PS[:, 0:1])), deps=[tt])
            psT_free = tc2
            inv_prev = tt
        inv_done = psT_free
        inv_tp_free = inv_done

        ps_free = [inv_done] + gate_free[0] + gate_free[1] + h_prev
        wpT4_free = [wT_free[i][-1] for i in range(4)]
        for sg in range(NSG):
            last_mm = None
            for k in range(NPAIR):
                wpar = k % 4
                tw = P.dma("act" if k % 2 == 0 else "pool",
                           ("dt%d" if k % 2 == 0 else "dp%d") % wpar,
                           (lambda e, k=k, wpar=wpar: e.dma_start(
                               out=wpT4[wpar][:, :],
                               in_=wptdram[k * 128:(k + 1) * 128, :])),
                           deps=[wpT4_free[wpar]] +
                           (wpq_done if sh == 0 and sg == 0 and k < 4
                            else []))
                def mmp(e, sg=sg, k=k, wpar=wpar):
                    ii = None
                    for sc in range(SCP):
                        row = sg * SCP + sc
                        for dt in range(4):
                            ii = e.matmul(
                                PS[:, sc * 2048 + dt * 512:
                                   sc * 2048 + (dt + 1) * 512],
                                Hbig[:, k * SHALF + row * 128:
                                     k * SHALF + row * 128 + 128],
                                wpT4[wpar][:, dt * 512:(dt + 1) * 512],
                                start=(k == 0), stop=(k == NPAIR - 1))
                    return ii
                tm = P.op("pe", mmp,
                          deps=[tw, hbig_toks[k]] +
                          (ps_free if k == 0 else []))
                wpT4_free[wpar] = tm
                last_mm = tm
            nps = []
            for sc in range(SCP):
                row = sg * SCP + sc
                grow = srow0 + row
                par = row % 2
                ta = P.dma("sync", "dal%d" % par,
                           (lambda e, grow=grow, par=par: e.dma_start(
                               out=acct[par][:, :],
                               in_=accdram[grow * 128:(grow + 1) * 128, :])),
                           deps=[p6_done, p6_done2, acct_free[par]])
                tx = P.dma("pool", "dxl%d" % par,
                           (lambda e, grow=grow, par=par: e.dma_start(
                               out=xt_[par][:, :],
                               in_=x_kv[grow * 128:(grow + 1) * 128, :])),
                           deps=[xtf_free[par]] +
                           ([p6_done, p6_done2] if sh == 0 else []))
                def ev1(e, sc=sc, grow=grow, par=par):
                    return e.scalar_tensor_tensor(
                        ost[par][:, :], PS[:, sc * 2048: sc * 2048 + D],
                        invcol[:, grow:grow + 1], acct[par][:, :],
                        AL.mult, AL.add)
                t1 = P.op("dve", ev1,
                          deps=[last_mm, ta, inv_done, ost_free[par]])
                acct_free[par] = t1
                def ev2(e, par=par):
                    return e.tensor_tensor(ost[par][:, :], ost[par][:, :],
                                           xt_[par][:, :], AL.add)
                t2 = P.op("dve", ev2, deps=[t1, tx])
                xtf_free[par] = t2
                def sto(e, grow=grow, par=par):
                    return e.dma_start(
                        out=out_d[grow * 128:(grow + 1) * 128, :],
                        in_=ost[par][:, :])
                ts_ = P.dma("sync", "do%d" % par, sto, deps=[t2])
                ost_free[par] = ts_
                out_toks.append(ts_)
                nps.append(t1)
            ps_free = nps

    with nc.Block() as block:
        sems = {}
        guards = []
        for s in P.sem_names:
            g = nc.semaphore(s)
            sems[s] = g.__enter__()
            guards.append(g)

        fin0 = P.counts.get("do0", 0)
        fin1 = P.counts.get("do1", 0)

        @block.tensor
        def _(t):
            P.run("pe", t, sems)

        @block.scalar
        def _(s):
            P.run("act", s, sems)

        @block.vector
        def _(v):
            P.run("dve", v, sems)

        @block.gpsimd
        def _(g):
            P.run("pool", g, sems)

        @block.sync
        def _(sy):
            P.run("sync", sy, sems)
            if fin0:
                sy.wait_ge(sems["do0"], fin0)
            if fin1:
                sy.wait_ge(sems["do1"], fin1)

    return nc



def host_prep(inputs, T):
    TQ = T // 2
    RR = T // 32
    NCH = TQ // 128
    x = np.ascontiguousarray(inputs["L0_attn_x"][:, :T, :])
    cc_i = np.arange(16)
    rr_i = np.arange(RR)
    sig = np.arange(TQ)
    s_loc = 16 * (sig % RR) + sig // RR
    mask1 = np.where(s_loc[:, None] <= s_loc[None, :], 0.0, NEG)
    mask1 = mask1.reshape(NCH, 128, TQ).astype(BF)
    ident = np.eye(128, dtype=np.float32)
    ones_col = np.ones((128, 1), dtype=BF)
    ones_row = np.ones((1, 128), dtype=np.float32)
    bq_t = np.ascontiguousarray(
        inputs["L0_attn_bq"].reshape(16, 128).T.astype(np.float32))
    bk_t = np.ascontiguousarray(
        inputs["L0_attn_bk"].reshape(16, 128).T.astype(np.float32))
    bv_r = np.ascontiguousarray(
        inputs["L0_attn_bv"].reshape(1, D).astype(np.float32))
    bo_r = np.ascontiguousarray(
        inputs["L0_attn_bo"].reshape(1, D).astype(np.float32))
    own_tok = {}
    for half in (0, 1):
        tok = (T // 2) * half + 16 * rr_i[None, :] + cc_i[:, None]
        own_tok[half] = tok.reshape(-1)
    maps = []
    for core in range(8):
        b, half = core // 2, core % 2
        own, oth = own_tok[half], own_tok[1 - half]
        x_kv = np.ascontiguousarray(x[b][np.concatenate([own, oth])])
        qrows = np.array([h * (T // 16) + RR * half + rr
                          for h in range(16) for rr in range(RR)])
        x_q = np.ascontiguousarray(x[b][qrows])
        p2b = np.full((128, 1), SHIFT + (NEG if half == 0 else 0.0),
                      dtype=np.float32)
        maps.append({
            "x_kv": x_kv, "x_q": x_q,
            "Wq": inputs["L0_attn_Wq"], "Wk": inputs["L0_attn_Wk"],
            "Wv": inputs["L0_attn_Wv"], "Wo": inputs["L0_attn_Wo"],
            "bq_t": bq_t, "bk_t": bk_t, "bv_r": bv_r, "bo_r": bo_r,
            "wgu": inputs["L0_ffn_w_gate_up"],
            "wp": inputs["L0_ffn_w_proj"],
            "mask1": mask1, "pass2b": p2b, "ident": ident,
            "ident_b": ident.astype(BF),
            "ones_col": ones_col, "ones_row": ones_row,
        })
    return maps, own_tok


def host_gather(results, own_tok, T, B=4):
    out = np.empty((B, T, D), dtype=np.float32)
    for core in range(8):
        b, half = core // 2, core % 2
        out[b][own_tok[half]] = results[core]["out"]
    return out


def kernel(**inputs):
    T = 2048
    nc = build_nc(T)
    in_maps, own_tok = host_prep(inputs, T)
    res = run_bass_kernel_spmd(nc, in_maps, core_ids=list(range(8)))
    return host_gather(res.results, own_tok, T)
